# version 16
# baseline (speedup 1.0000x reference)
import numpy as np
import ml_dtypes
import contextlib
import concourse.bass as bass
import concourse.mybir as mybir
from concourse.bass_utils import run_bass_kernel_spmd

F32 = mybir.dt.float32
BF16 = mybir.dt.bfloat16
BF = ml_dtypes.bfloat16
Alu = mybir.AluOpType
Act = mybir.ActivationFunctionType
AX = mybir.AxisListType.X

FULL = dict(D=2048, H=5632, NH=16, T_core=2048, TC=1024, n_cores=8)
NEG = -1.0e9
EPS_Q = 1e-8
EPS_LN = 1e-5


class Sem:
    def __init__(self, stack, nc, name):
        self.h = stack.enter_context(nc.semaphore(name))
        self.v = 0


def build_program(P):
    D, H, NH, T_core, TC = P["D"], P["H"], P["NH"], P["T_core"], P["TC"]
    KD, KH = D // 128, H // 128
    G = 128 // NH
    NG = TC // G
    NCH = T_core // TC
    half = TC // 2
    NQT = 4
    KHQ = KH // NQT
    W3C = H // NQT
    NW = float(H * D)
    NW512 = min(512, TC)
    assert D % 128 == 0 and H % 128 == 0 and KH % NQT == 0
    assert D // NH == 128, "head dim must be 128"
    assert TC % G == 0 and TC % 2 == 0 and T_core % TC == 0

    nc = bass.Bass("TRN2", target_bir_lowering=False)

    d_x = nc.dram_tensor("x", [KD, 128, T_core], F32, kind="ExternalInput").ap()
    d_wq = nc.dram_tensor("wq", [KD, 128, D], F32, kind="ExternalInput").ap()
    d_wk = nc.dram_tensor("wk", [KD, 128, D], F32, kind="ExternalInput").ap()
    d_wv = nc.dram_tensor("wv", [KD, 128, D], F32, kind="ExternalInput").ap()
    d_wo = nc.dram_tensor("wo", [KD, 128, D], F32, kind="ExternalInput").ap()
    d_w1 = nc.dram_tensor("w1", [KH, 128, D], F32, kind="ExternalInput").ap()
    d_w2 = nc.dram_tensor("w2", [KH, 128, D], F32, kind="ExternalInput").ap()
    d_w3 = nc.dram_tensor("w3", [KD, 128, H], F32, kind="ExternalInput").ap()
    d_mask = nc.dram_tensor("mask", [128, 128], F32, kind="ExternalInput").ap()
    d_id = nc.dram_tensor("idm", [128, 128], BF16, kind="ExternalInput").ap()
    d_onesb = nc.dram_tensor("onesb", [128, 1], BF16, kind="ExternalInput").ap()
    d_onescf = nc.dram_tensor("onescf", [128, 1], F32, kind="ExternalInput").ap()
    d_onesr = nc.dram_tensor("onesr", [1, 128], F32, kind="ExternalInput").ap()
    d_out = nc.dram_tensor("out", [KD, 128, T_core], F32, kind="ExternalOutput").ap()

    TCB = TC * 2
    qsz = NG * 256
    xn_off = 0
    s_off = xn_off + KD * TCB
    q_off = s_off + KD * TCB
    k_off = q_off + qsz
    v_off = k_off + qsz
    ao_off = v_off + qsz
    hid_off = q_off
    hid_sz = KH * TCB
    assert hid_sz <= 4 * qsz
    wstr_off = hid_off + hid_sz
    wstr_sz = 3 * D * 4
    qstr_off = wstr_off + wstr_sz
    qstr_sz = max(2 * D * 2, 3 * W3C * 2)
    after = max(ao_off + qsz, qstr_off + qstr_sz)
    sqg_off = after
    sqg_sz = 2 * TCB
    attws_off = sqg_off
    attws_sz = 2 * (512 + 4 * 256) + 4 * 32
    misc_off = sqg_off + max(sqg_sz, attws_sz, 4096)
    misc_sz = 512 + 256 + 64 + 64 + 512 + 6 * 32 + 6 * 32 + \
        4 * (KH * 2 + NQT * KD) + 3 * 32 + 512
    qsb_off = misc_off + misc_sz
    qsb_sz = max(2 * D * 2, 2 * W3C * 2, 4 * half * 4)
    total = qsb_off + qsb_sz
    stg_sz = 4 * W3C * 4
    if stg_sz <= KD * TCB:
        w3sl_off = xn_off
    else:
        w3sl_off = total
        total += stg_sz
    xstg_off = ao_off
    ostg_off = xstg_off + 2 * half * 4
    assert ostg_off + 2 * half * 4 <= qstr_off or qstr_off < ao_off
    if 4 * D * 4 <= KD * TCB:
        gam1_off = s_off
        gam2_off = s_off + 2 * D * 4
    else:
        gam1_off = total
        gam2_off = total + 2 * D * 4
        total += 4 * D * 4

    arena = nc.alloc_sbuf_tensor("arena", [128, total], mybir.dt.uint8)
    A0 = nc.lookup_mloc(arena).addr
    assert A0 + total <= nc.sbuf_top, f"sbuf overflow: {A0}+{total} > {nc.sbuf_top}"

    def at(name, off, shape, dtype):
        return nc.alloc_sbuf_tensor_at(name, shape, dtype, offset=A0 + off)

    xn = at("xn", xn_off, [128, KD, TC], BF16)
    s_sb = at("s_sb", s_off, [128, KD, TC], BF16)
    q_sb = at("q_sb", q_off, [128, NG, 128], BF16)
    k_sb = at("k_sb", k_off, [128, NG, 128], BF16)
    v_sb = at("v_sb", v_off, [128, NG, 128], BF16)
    ao_sb = at("ao_sb", ao_off, [128, NH, TC], BF16)
    hid = at("hid", hid_off, [128, KH, TC], BF16)
    wsl = [at(f"wsl{i}", wstr_off + i * D * 4, [128, D], F32) for i in range(3)]
    q12 = [at(f"q12_{i}", qstr_off + i * D * 2, [128, D], BF16) for i in range(2)]
    q3sl = [at(f"q3_{i}", qstr_off + i * W3C * 2, [128, W3C], BF16) for i in range(3)]
    w3sl = [at(f"w3sl{i}", w3sl_off + i * W3C * 4, [128, W3C], F32) for i in range(4)]
    xstg_f = [at(f"xstgf{i}", w3sl_off + i * TC * 4, [128, TC], F32) for i in range(2)]
    xstg = [at(f"xstg{i}", xstg_off + i * half * 4, [128, half], F32) for i in range(2)]
    ostg = [at(f"ostg{i}", ostg_off + i * half * 4, [128, half], F32) for i in range(2)]
    sqb = at("sqb", sqg_off, [128, TC], BF16)
    sig = [at(f"sig{i}", sqg_off + i * TCB, [128, TC], BF16) for i in range(2)]
    gsb = [at(f"gsb{i}", sqg_off + i * TCB, [128, TC], BF16) for i in range(2)]
    tmpw = [at(f"tmpw{i}", attws_off + i * 512, [128, 128], F32) for i in range(2)]
    Ab = [at(f"Ab{i}", attws_off + 1024 + i * 256, [128, 128], BF16) for i in range(2)]
    An = [at(f"An{i}", attws_off + 1536 + i * 256, [128, 128], BF16) for i in range(2)]
    ATs = [at(f"ATs{i}", attws_off + 2048 + i * 256, [128, 128], BF16) for i in range(2)]
    VTs = [at(f"VTs{i}", attws_off + 2560 + i * 256, [128, 128], BF16) for i in range(2)]
    rs = [at(f"rs{i}", attws_off + 3072 + i * 32, [128, 1], F32) for i in range(2)]
    rr = [at(f"rr{i}", attws_off + 3136 + i * 32, [128, 1], F32) for i in range(2)]
    mo = misc_off
    mask_sb = at("mask_sb", mo, [128, 128], F32); mo += 512
    id_sb = at("id_sb", mo, [128, 128], BF16); mo += 256
    onesb_sb = at("onesb_sb", mo, [128, 1], BF16); mo += 64
    onescf_sb = at("onescf_sb", mo, [128, 1], F32); mo += 64
    onesr_sb = at("onesr_sb", mo, [1, 128], F32); mo += 512
    bpos = []
    bneg = []
    for m in range(3):
        bpos.append(at(f"bpos{m}", mo, [128, 1], F32)); mo += 32
        bneg.append(at(f"bneg{m}", mo, [128, 1], F32)); mo += 32
    def _al(v):
        return (v + 31) & ~31
    gcols1 = at("gcols1", mo, [128, KH], F32); mo = _al(mo + KH * 4)
    gcols2 = at("gcols2", mo, [128, KH], F32); mo = _al(mo + KH * 4)
    gcols3 = at("gcols3", mo, [128, KD], F32); mo = _al(mo + KD * 4)
    gtmp = at("gtmp", mo, [128, 16], F32); mo = _al(mo + 64)
    g128 = []
    for m in range(3):
        g128.append(at(f"g128_{m}", mo, [128, 1], F32)); mo += 32
    gbv = []
    for m in range(3):
        gbv.append(at(f"gbv{m}", mo, [1, 1], F32)); mo += 32
    assert mo <= qsb_off
    qsbt = [at(f"qsbt{i}", qsb_off + i * D * 2, [128, D], BF16) for i in range(2)]
    q3sbt = [at(f"q3sbt{i}", qsb_off + i * W3C * 2, [128, W3C], BF16) for i in range(2)]
    pcst = [at(f"pcst{i}", ao_off + i * D * 2, [128, D], BF16) for i in range(4)]
    wcstw = [at(f"wcstw{i}", k_off + i * D * 2, [128, D], BF16) for i in range(4)]
    assert 4 * D * 2 <= qsz
    stv = [at(f"stv{i}", qsb_off + i * half * 4, [1, half], F32) for i in range(4)]
    st_negm, st_esq, st_tbuf, st_a = stv
    NR3 = min(16, qsz // 2048)
    gpcw = [at(f"gpcw{i}", q_off + i * 2048, [128, 512], F32) for i in range(NR3)]
    gs1 = [at(f"gs1_{i}", gam1_off + i * D * 4, [128, D], F32) for i in range(2)]
    gs2 = [at(f"gs2_{i}", gam2_off + i * D * 4, [128, D], F32) for i in range(2)]

    stack = contextlib.ExitStack()
    prog = {"sync": [], "tensor": [], "vector": [], "scalar": [], "gpsimd": []}

    def mksem(name):
        return Sem(stack, nc, name)

    def emit(e, fn):
        prog[e].append(fn)

    def waits(e, wl):
        for (sm, val) in wl:
            if val > 0:
                emit(e, lambda eng, sm=sm, val=val: eng.wait_ge(sm.h, val))

    def dma(e, out, in_, sm, wl=()):
        waits(e, wl)
        sm.v += 16
        emit(e, lambda eng, out=out, in_=in_, sm=sm:
             eng.dma_start(out=out, in_=in_).then_inc(sm.h, 16))
        return sm.v

    def op(e, fn, sm=None, wl=(), drain=False):
        if drain:
            emit(e, lambda eng: eng.drain())
        waits(e, wl)
        if sm is not None:
            sm.v += 1
            emit(e, lambda eng, fn=fn, sm=sm: fn(eng).then_inc(sm.h, 1))
        else:
            emit(e, lambda eng, fn=fn: fn(eng))

    sGPW = [mksem(f"gpw{i}") for i in range(NR3)]
    sGS1 = [mksem(f"g1s{i}") for i in range(2)]
    sGS2 = [mksem(f"g2s{i}") for i in range(2)]
    sGRU = mksem("gru")
    sGR = mksem("gr")
    sBI = mksem("bi")
    sLXt = [mksem(f"lxt{i}") for i in range(KD)]
    sLXf = [mksem(f"lxf{i}") for i in range(2)]
    sLXc = mksem("lxc")
    sSQ = mksem("lsq")
    sLST = mksem("lst")
    sLSV = mksem("lsv")
    sLBC = mksem("lbc")
    sLAP = mksem("lap")
    sPCp = [mksem(f"pcp{i}") for i in range(4)]
    sWCp = [mksem(f"wcp{i}") for i in range(4)]
    sPMM = mksem("pmm")
    sPEV = mksem("pev")
    sS, sSTT, sEXP, sNORM = mksem("as"), mksem("astt"), mksem("aexp"), mksem("anorm")
    sAT, sVT, sATC, sVTC = mksem("aat"), mksem("avt"), mksem("aatc"), mksem("avtc")
    sAV, sAOE = mksem("aav"), mksem("aaoe")
    sWMM = mksem("wmm")
    sWEV = mksem("wev")
    sWSL = [mksem(f"wsl_s{i}") for i in range(3)]
    sSGN = mksem("sgn")
    sQA = mksem("qa")
    sF1MM = mksem("f1mm")
    sSIG = mksem("sigm")
    sHID = mksem("hidm")
    sW3S = [mksem(f"w3s_s{i}") for i in range(4)]
    sSGN3 = mksem("sgn3")
    sQA3 = mksem("qa3")
    sF2MM = mksem("f2mm")
    sSILU = mksem("silu")
    sXLs = [mksem(f"xls{i}") for i in range(2)]
    sRES = mksem("res")
    sOUTs = [mksem(f"outs{i}") for i in range(2)]
    sCONST = mksem("cst")

    dma("sync", mask_sb[:], d_mask, sCONST)
    dma("sync", id_sb[:], d_id, sCONST)
    dma("sync", onesb_sb[:], d_onesb, sCONST)
    dma("sync", onescf_sb[:], d_onescf, sCONST)
    dma("sync", onesr_sb[:], d_onesr, sCONST)
    CONST_ALL = sCONST.v

    sGRv = mksem("grv")
    q1info = []
    for j in range(KH):
        q1info.append((d_w1[j], gs1[j % 2], sGS1[j % 2], gcols1[j:j + 1] if False else gcols1, j))
        q1info.append((d_w2[j], gs2[j % 2], sGS2[j % 2], gcols2, j))
    NQ1 = len(q1info)
    GRU_FULL = NQ1
    Q1 = {"pop": 0, "load": 0}
    Q1V = {}

    def _q1load(extra_wl):
        k = Q1["load"]
        if k >= NQ1:
            return
        ap, buf, sem, _, _ = q1info[k]
        wl = list(extra_wl) + ([(sGRU, k - 3)] if k - 3 > 0 else [])
        Q1V[k] = dma("sync", buf[:], ap, sem, wl=wl)
        Q1["load"] += 1

    def pop_q1(n, entry_wl=()):
        for _ in range(n):
            k = Q1["pop"]
            if k >= NQ1:
                return
            while Q1["load"] < min(k + 4, NQ1):
                _q1load(entry_wl)
            ap, buf, sem, gc, j = q1info[k]
            op("vector", lambda e, buf=buf, gc=gc, j=j: e.tensor_reduce(
                out=gc[:, j:j + 1], in_=buf[:], axis=AX, op=Alu.add,
                apply_absolute_value=True), sm=sGRU, wl=[(sem, Q1V[k])])
            Q1["pop"] += 1
    ginfo = []
    npp3 = H // 512
    for wd, gc, npp, nblk in ((d_w3, gcols3, npp3, KD),):
        for j in range(nblk):
            for p in range(npp):
                ginfo.append((wd[j, :, p * 512:(p + 1) * 512], p,
                              (gc, j, npp) if p == npp - 1 else None))
    NPIECES = len(ginfo)
    GST = {"pop": 0, "load": 0}
    GV = {}

    def _gload(extra_wl):
        k = GST["load"]
        if k >= NPIECES:
            return
        wl = list(extra_wl) + ([(sGRv, k - NR3 + 1)] if k - NR3 + 1 > 0 else [])
        GV[k] = dma("sync", gpcw[k % NR3][:], ginfo[k][0], sGPW[k % NR3], wl=wl)
        GST["load"] += 1

    def pop_gred(n, entry_wl=(), load_limit=None):
        lim = NPIECES if load_limit is None else min(load_limit, NPIECES)
        for _ in range(n):
            k = GST["pop"]
            if k >= NPIECES:
                return
            while GST["load"] < min(k + NR3 - 1, lim):
                _gload(entry_wl)
            _, p, bs = ginfo[k]
            op("vector", lambda e, k=k, p=p: e.tensor_reduce(
                out=gtmp[:, p:p + 1], in_=gpcw[k % NR3][:], axis=AX, op=Alu.add,
                apply_absolute_value=True), sm=sGRv,
               wl=[(sGPW[k % NR3], GV[k])], drain=(p == 0))
            if bs is not None:
                gc, j, npp = bs
                op("vector", lambda e, gc=gc, j=j, npp=npp: e.tensor_reduce(
                    out=gc[:, j:j + 1], in_=gtmp[:, 0:npp], axis=AX, op=Alu.add),
                   drain=True)
            GST["pop"] += 1

    def gamma_finals():
        op("vector", lambda e: e.tensor_reduce(out=g128[0][:], in_=gcols1[:], axis=AX,
                                               op=Alu.add), sm=sGR, drain=True,
           wl=[(s, s.v) for s in sGPW] + [(sGRv, sGRv.v),
               (sGS1[0], sGS1[0].v), (sGS1[1], sGS1[1].v),
               (sGS2[0], sGS2[0].v), (sGS2[1], sGS2[1].v),
               (sGRU, GRU_FULL)])
        op("vector", lambda e: e.tensor_reduce(out=g128[1][:], in_=gcols2[:], axis=AX,
                                               op=Alu.add), sm=sGR)
        op("vector", lambda e: e.tensor_reduce(out=g128[2][:], in_=gcols3[:], axis=AX,
                                               op=Alu.add), sm=sGR)
    GR_FULL = 3

    prev = dict(SGN3=0, RES=0, OUT0=0, OUT1=0, QA3=0, QA=0, WMM=0, SILU=0, AOE=0)
    BI_FULL = [0]

    for c in range(NCH):
        c0 = c * TC

        with contextlib.ExitStack() as ps:
            pssum = [ps.enter_context(nc.psum_tensor(f"pssum{c}_{h}", [1, half], F32)) for h in range(2)]
            pssq = [ps.enter_context(nc.psum_tensor(f"pssq{c}_{h}", [1, half], F32)) for h in range(2)]
            pa = [ps.enter_context(nc.psum_tensor(f"pa{c}_{h}", [128, half], F32)) for h in range(2)]
            pb = [ps.enter_context(nc.psum_tensor(f"pb{c}_{h}", [128, half], F32)) for h in range(2)]

            bSQ, bLST, bLSV, bLBC, bLAP = sSQ.v, sLST.v, sLSV.v, sLBC.v, sLAP.v
            lxv = {}
            n_w3_tiles = (4 * W3C * 4 + TCB - 1) // TCB if w3sl_off == xn_off else 0
            for ti in range(KD):
                wl = []
                if c > 0:
                    wl = [(sF1MM, prev["F1MM"])]
                    if ti < n_w3_tiles:
                        wl.append((sSGN3, prev["SGN3"]))
                lxv[ti] = dma("gpsimd", xn[:, ti, :], d_x[ti, :, c0:c0 + TC],
                              sLXt[ti], wl=wl)
            for ti in range(KD):
                if lxv[ti] is None:
                    wl = [(sLXc, bLXc + ti - (KD - NSPL) + 1), (sLST, bLST + ti)]
                else:
                    wl = [(sLXt[ti], lxv[ti]), (sLST, bLST + ti)]
                if ti == 0 and c > 0:
                    wl.append((sRES, prev["RES"]))
                    wl.append((sGR, GR_FULL))
                op("scalar", lambda e, ti=ti: e.square(out=sqb[:], in_=xn[:, ti, :]),
                   sm=sSQ, wl=wl)
                if lxv[ti] is None:
                    wl = [(sLXc, bLXc + ti - (KD - NSPL) + 1), (sSQ, bSQ + ti + 1)]
                else:
                    wl = [(sLXt[ti], lxv[ti]), (sSQ, bSQ + ti + 1)]
                if ti == 0:
                    if c == 0:
                        wl.append((sCONST, CONST_ALL))
                    else:
                        wl.append((sSILU, prev["SILU"]))
                        wl.append((sAOE, prev["AOE"]))
                        wl.append((sBI, BI_FULL[0]))
                waits("tensor", wl)
                for h in range(2):
                    op("tensor", lambda e, ti=ti, h=h: e.matmul(
                        out=pssum[h][:], lhsT=onesb_sb[:], rhs=xn[:, ti, h * half:(h + 1) * half],
                        start=(ti == 0), stop=(ti == KD - 1)))
                for h in range(2):
                    op("tensor", lambda e, ti=ti, h=h: e.matmul(
                        out=pssq[h][:], lhsT=onesb_sb[:], rhs=sqb[:, h * half:(h + 1) * half],
                        start=(ti == 0), stop=(ti == KD - 1)),
                       sm=(sLST if h == 1 else None))
            for h in range(2):
                waits("vector", [(sLST, bLST + KD)])
                op("vector", lambda e, h=h: e.tensor_scalar(
                    out=st_negm[:], in0=pssum[h][:], scalar1=-1.0 / D, scalar2=None, op0=Alu.mult),
                   drain=True)
                op("vector", lambda e, h=h: e.tensor_scalar(
                    out=st_esq[:], in0=pssq[h][:], scalar1=1.0 / D, scalar2=None, op0=Alu.mult),
                   drain=True)
                op("vector", lambda e: e.tensor_tensor(
                    out=st_tbuf[:], in0=st_negm[:], in1=st_negm[:], op=Alu.mult), drain=True)
                op("vector", lambda e: e.tensor_tensor(
                    out=st_esq[:], in0=st_esq[:], in1=st_tbuf[:], op=Alu.subtract), drain=True)
                op("vector", lambda e: e.tensor_scalar(
                    out=st_esq[:], in0=st_esq[:], scalar1=EPS_LN, scalar2=None, op0=Alu.add),
                   drain=True)
                op("vector", lambda e: e.reciprocal(out=st_tbuf[:], in_=st_esq[:]), sm=sLSV,
                   drain=True)
                op("scalar", lambda e: e.sqrt(out=st_a[:], in_=st_tbuf[:]),
                   sm=sLSV, wl=[(sLSV, bLSV + 3 * h + 1), (sLBC, bLBC + 2 * h)])
                op("vector", lambda e: e.tensor_tensor(
                    out=st_tbuf[:], in0=st_negm[:], in1=st_a[:], op=Alu.mult),
                   sm=sLSV, wl=[(sLSV, bLSV + 3 * h + 2)], drain=True)
                waits("tensor", [(sLSV, bLSV + 3 * h + 3)])
                op("tensor", lambda e, h=h: e.matmul(
                    out=pa[h][:], lhsT=onesr_sb[:], rhs=st_a[:], start=True, stop=True),
                   sm=sLBC)
                op("tensor", lambda e, h=h: e.matmul(
                    out=pb[h][:], lhsT=onesr_sb[:], rhs=st_tbuf[:], start=True, stop=True),
                   sm=sLBC)
                if h == 0:
                    waits("vector", [(sLBC, bLBC + 2)])
            for ti in range(KD):
                for h in range(2):
                    waits("vector", [(sLBC, bLBC + 2 * (h + 1))])
                    op("vector", lambda e, ti=ti, h=h: e.tensor_tensor(
                        out=sqb[:, 0:half], in0=xn[:, ti, h * half:(h + 1) * half],
                        in1=pa[h][:], op=Alu.mult), drain=True)
                    op("vector", lambda e, ti=ti, h=h: e.tensor_tensor(
                        out=xn[:, ti, h * half:(h + 1) * half], in0=sqb[:, 0:half],
                        in1=pb[h][:], op=Alu.add), sm=sLAP, drain=True)
            LAP_FULL = sLAP.v

        with contextlib.ExitStack() as ps:
            pacc = [ps.enter_context(nc.psum_tensor(f"pacc{c}_{i}", [128, TC], F32)) for i in range(2)]
            bPMM, bPEV = sPMM.v, sPEV.v
            LBC_FULL = sLBC.v
            b = 0
            for kind, wd in (("q", d_wq), ("k", d_wk), ("v", d_wv)):
                for j in range(KD):
                    wl = [(sPMM, bPMM + b - 3)]
                    if b <= 3:
                        wl.append((sLBC, LBC_FULL))
                        if c > 0:
                            wl.append((sQA3, prev["QA3"]))
                            wl.append((sQA, prev["QA"]))
                            wl.append((sRES, prev["RES"]))
                            wl.append((sOUTs[0], prev["OUT0"]))
                            wl.append((sOUTs[1], prev["OUT1"]))
                            wl.append((sAOE, prev["AOE"]))
                    vc = dma("gpsimd", pcst[b % 4][:], wd[j], sPCp[b % 4], wl=wl)
                    wl = [(sPCp[b % 4], vc), (sPEV, bPEV + b - 1)]
                    if b == 0:
                        wl.append((sLAP, LAP_FULL))
                    waits("tensor", wl)
                    for i in range(KD):
                        for n0 in range(0, TC, NW512):
                            op("tensor", lambda e, b=b, i=i, n0=n0: e.matmul(
                                out=pacc[b % 2][:, n0:n0 + NW512],
                                lhsT=pcst[b % 4][:, i * 128:(i + 1) * 128],
                                rhs=xn[:, i, n0:n0 + NW512],
                                start=(i == 0), stop=(i == KD - 1)),
                               sm=(sPMM if (i == KD - 1 and n0 + NW512 >= TC) else None))
                    dst = {"q": q_sb, "k": k_sb, "v": v_sb}[kind]
                    op("vector", lambda e, b=b, j=j, dst=dst: e.tensor_copy(
                        out=dst[:, :, j * G:(j + 1) * G],
                        in_=pacc[b % 2][:].rearrange("p (g t) -> p g t", t=G)),
                       sm=sPEV, wl=[(sPMM, bPMM + b + 1)])
                    if c == 0:
                        pop_q1(2)
                    b += 1
            PMM_FULL = sPMM.v

        with contextlib.ExitStack() as ps:
            pS = [ps.enter_context(nc.psum_tensor(f"pS{c}_{i}", [128, 128], F32)) for i in range(2)]
            pAT = [ps.enter_context(nc.psum_tensor(f"pAT{c}_{i}", [128, 128], BF16)) for i in range(2)]
            pVT = [ps.enter_context(nc.psum_tensor(f"pVT{c}_{i}", [128, 128], BF16)) for i in range(2)]
            pAV = [ps.enter_context(nc.psum_tensor(f"pAV{c}_{i}", [128, 128], F32)) for i in range(2)]
            bS, bSTT, bEXP, bNORM = sS.v, sSTT.v, sEXP.v, sNORM.v
            bAT, bVT, bATC, bVTC = sAT.v, sVT.v, sATC.v, sVTC.v
            bAV, bAOE = sAV.v, sAOE.v
            PEV_FULL = sPEV.v
            ISQ = float(1.0 / np.sqrt(128.0))

            for g in range(NG + 7):
                if g < NG:
                    gg = g
                    wl = [(sSTT, bSTT + gg - 1)]
                    if gg == 0:
                        wl.append((sPEV, PEV_FULL))
                    op("tensor", lambda e, gg=gg: e.matmul(
                        out=pS[gg % 2][:], lhsT=q_sb[:, gg, :], rhs=k_sb[:, gg, :],
                        start=True, stop=True), sm=sS, wl=wl)
                if 0 <= g - 1 < NG:
                    gg = g - 1
                    wl = [(sS, bS + gg + 1), (sEXP, bEXP + gg - 1)]
                    if gg == 0 and c == 0:
                        wl.append((sCONST, CONST_ALL))
                        wl.append((sGRv, sGRv.v))
                    op("vector", lambda e, gg=gg: e.scalar_tensor_tensor(
                        out=tmpw[gg % 2][:], in0=pS[gg % 2][:], scalar=ISQ,
                        in1=mask_sb[:], op0=Alu.mult, op1=Alu.add),
                       sm=sSTT, wl=wl)
                if 0 <= g - 2 < NG:
                    gg = g - 2
                    wl = [(sSTT, bSTT + gg + 1), (sNORM, bNORM + gg - 1)]
                    if gg == 0 and c == 0:
                        wl.append((sGRv, sGRv.v))
                    op("scalar", lambda e, gg=gg: e.activation(
                        out=Ab[gg % 2][:], in_=tmpw[gg % 2][:], func=Act.Exp,
                        accum_out=rs[gg % 2][:]),
                       sm=sEXP, wl=wl)
                if 0 <= g - 3 < NG:
                    gg = g - 3
                    waits("vector", [(sEXP, bEXP + gg + 1), (sAT, bAT + gg - 1)])
                    op("vector", lambda e, gg=gg: e.reciprocal(out=rr[gg % 2][:], in_=rs[gg % 2][:]))
                    op("vector", lambda e, gg=gg: e.tensor_scalar(
                        out=An[gg % 2][:], in0=Ab[gg % 2][:], scalar1=rr[gg % 2][:],
                        scalar2=None, op0=Alu.mult), sm=sNORM, drain=True)
                if 0 <= g - 4 < NG:
                    gg = g - 4
                    op("tensor", lambda e, gg=gg: e.transpose(
                        out=pAT[gg % 2][:], in_=An[gg % 2][:], identity=id_sb[:]),
                       sm=sAT, wl=[(sNORM, bNORM + gg + 1), (sATC, bATC + gg - 1)])
                    op("tensor", lambda e, gg=gg: e.transpose(
                        out=pVT[gg % 2][:], in_=v_sb[:, gg, :], identity=id_sb[:]),
                       sm=sVT, wl=[(sVTC, bVTC + gg - 1)])
                if 0 <= g - 5 < NG:
                    gg = g - 5
                    op("scalar", lambda e, gg=gg: e.copy(out=ATs[gg % 2][:], in_=pAT[gg % 2][:]),
                       sm=sATC, wl=[(sAT, bAT + gg + 1), (sAV, bAV + gg - 1)])
                    op("vector", lambda e, gg=gg: e.tensor_copy(out=VTs[gg % 2][:], in_=pVT[gg % 2][:]),
                       sm=sVTC, wl=[(sVT, bVT + gg + 1), (sAV, bAV + gg - 1)])
                if 0 <= g - 6 < NG:
                    gg = g - 6
                    op("tensor", lambda e, gg=gg: e.matmul(
                        out=pAV[gg % 2][:], lhsT=VTs[gg % 2][:], rhs=ATs[gg % 2][:],
                        start=True, stop=True),
                       sm=sAV, wl=[(sATC, bATC + gg + 1), (sVTC, bVTC + gg + 1),
                                   (sAOE, bAOE + gg - 1)])
                if 0 <= g - 7 < NG:
                    gg = g - 7
                    wl = [(sAV, bAV + gg + 1)]
                    if gg == 0 and c > 0:
                        wl.append((sWMM, prev["WMM"]))
                        wl.append((sRES, prev["RES"]))
                        wl.append((sOUTs[0], prev["OUT0"]))
                        wl.append((sOUTs[1], prev["OUT1"]))
                    op("scalar", lambda e, gg=gg: e.copy(
                        out=ao_sb[:, :, gg * G:(gg + 1) * G],
                        in_=pAV[gg % 2][:].rearrange("p (h t) -> p h t", t=G)),
                       sm=sAOE, wl=wl)
            S_FULL, VT_FULL, AOE_FULL = sS.v, sVT.v, sAOE.v
            EXP_FULL, NORM_FULL, AT_FULL, AV_FULL = sEXP.v, sNORM.v, sAT.v, sAV.v

        with contextlib.ExitStack() as ps:
            pacc = [ps.enter_context(nc.psum_tensor(f"wacc{c}_{i}", [128, TC], F32)) for i in range(2)]
            bWMM, bWEV = sWMM.v, sWEV.v
            for j in range(KD):
                wl = [(sWMM, bWMM + j - 3)]
                if j <= 3:
                    wl.append((sS, S_FULL))
                vc = dma("gpsimd", wcstw[j % 4][:], d_wo[j], sWCp[j % 4], wl=wl)
                wl = [(sWCp[j % 4], vc), (sWEV, bWEV + j - 1)]
                if j == 0:
                    wl.append((sAOE, AOE_FULL))
                waits("tensor", wl)
                for i in range(KD):
                    for n0 in range(0, TC, NW512):
                        op("tensor", lambda e, j=j, i=i, n0=n0: e.matmul(
                            out=pacc[j % 2][:, n0:n0 + NW512],
                            lhsT=wcstw[j % 4][:, i * 128:(i + 1) * 128],
                            rhs=ao_sb[:, i, n0:n0 + NW512],
                            start=(i == 0), stop=(i == KD - 1)),
                           sm=(sWMM if (i == KD - 1 and n0 + NW512 >= TC) else None))
                wl = [(sWMM, bWMM + j + 1)]
                if j == 0 and c == 0:
                    pop_q1(NQ1 + 1)
                    wl.append((sGRU, GRU_FULL))
                if j == 0 and c > 0:
                    wl.append((sOUTs[0], prev["OUT0"]))
                    wl.append((sOUTs[1], prev["OUT1"]))
                op("vector", lambda e, j=j: e.tensor_copy(
                    out=s_sb[:, j, :], in_=pacc[j % 2][:]), sm=sWEV, wl=wl)
                if c == 0:
                    pop_q1(NQ1 + 1)
                    aw = [(sEXP, EXP_FULL), (sNORM, NORM_FULL), (sAT, AT_FULL),
                          (sAV, AV_FULL), (sS, S_FULL)]
                    pop_gred(12 if j < KD - 1 else NPIECES + 1, entry_wl=aw)
                    if j == KD - 1:
                        gamma_finals()
            WMM_FULL = sWMM.v
            WEV_FULL = sWEV.v

            if c == 0:
                pgs = ps.enter_context(nc.psum_tensor("pgs", [1, 1], F32))
                pgb = ps.enter_context(nc.psum_tensor("pgb", [128, 1], F32))
                bBI = sBI.v
                for m in range(3):
                    op("tensor", lambda e, m=m: e.matmul(
                        out=pgs[:], lhsT=g128[m][:], rhs=onescf_sb[:], start=True, stop=True),
                       sm=sBI, wl=[(sGR, GR_FULL), (sBI, bBI + 4 * m)])
                    op("vector", lambda e, m=m: e.tensor_scalar(
                        out=gbv[m][:], in0=pgs[:], scalar1=0.5 / NW, scalar2=EPS_Q * 0.5,
                        op0=Alu.mult, op1=Alu.add), sm=sBI, wl=[(sBI, bBI + 4 * m + 1)])
                    op("tensor", lambda e, m=m: e.matmul(
                        out=pgb[:], lhsT=onesr_sb[:], rhs=gbv[m][:], start=True, stop=True),
                       sm=sBI, wl=[(sBI, bBI + 4 * m + 2)])
                    waits("vector", [(sBI, bBI + 4 * m + 3)])
                    op("vector", lambda e, m=m: e.tensor_copy(out=bpos[m][:], in_=pgb[:]))
                    op("vector", lambda e, m=m: e.tensor_scalar(
                        out=bneg[m][:], in0=bpos[m][:], scalar1=-1.0, scalar2=None,
                        op0=Alu.mult), sm=sBI, drain=True)
                BI_FULL[0] = sBI.v

        with contextlib.ExitStack() as ps:
            px1 = [ps.enter_context(nc.psum_tensor(f"px1_{c}_{i}", [128, TC], F32)) for i in range(2)]
            px2 = [ps.enter_context(nc.psum_tensor(f"px2_{c}_{i}", [128, TC], F32)) for i in range(2)]
            bSGN, bQA, bF1MM, bSIG, bHID = \
                sSGN.v, sQA.v, sF1MM.v, sSIG.v, sHID.v

            def f1_epilogue(j):
                wl = [(sF1MM, bF1MM + 2 * j + 1), (sHID, bHID + j - 1)]
                if j <= 1 and c == 0:
                    wl.append((sGR, GR_FULL))
                if j <= 1 and c > 0:
                    wl.append((sRES, prev["RES"]))
                op("scalar", lambda e, j=j: e.activation(
                    out=sig[j % 2][:], in_=px1[j % 2][:], func=Act.Sigmoid, scale=0.5),
                   sm=sSIG, wl=wl)
                op("vector", lambda e, j=j: e.tensor_tensor(
                    out=hid[:, j, :], in0=sig[j % 2][:], in1=px2[j % 2][:], op=Alu.mult),
                   sm=sHID, wl=[(sSIG, bSIG + j + 1), (sF1MM, bF1MM + 2 * j + 2)])

            for j in range(KH):
                wl1 = [(sSGN, bSGN + 4 * j - 4)]
                wl2 = [(sSGN, bSGN + 4 * j - 2)]
                if j == 0:
                    entry = [(sS, S_FULL), (sVT, VT_FULL), (sWMM, WMM_FULL),
                             (sGR, GR_FULL)]
                    if c > 0:
                        entry += [(sRES, prev["RES"]), (sOUTs[0], prev["OUT0"]),
                                  (sOUTs[1], prev["OUT1"])]
                    wl1 = entry
                    wl2 = entry
                vl1 = dma("sync", wsl[(2 * j) % 3][:], d_w1[j], sWSL[(2 * j) % 3], wl=wl1)
                vl2 = dma("scalar", wsl[(2 * j + 1) % 3][:], d_w2[j], sWSL[(2 * j + 1) % 3], wl=wl2)
                for t, ldw in ((2 * j, (sWSL[(2 * j) % 3], vl1)),
                               (2 * j + 1, (sWSL[(2 * j + 1) % 3], vl2))):
                    wl = [ldw, (sF1MM, bF1MM + t - 1), (sQA, bQA + t - 1)]
                    if t == 0:
                        wl.append((sBI, bBI + 8 if c == 0 else BI_FULL[0]))
                        wl.append((sWMM, WMM_FULL))
                        wl.append((sS, S_FULL))
                        wl.append((sVT, VT_FULL))
                    bias_p = bpos[0] if t % 2 == 0 else bpos[1]
                    bias_n = bneg[0] if t % 2 == 0 else bneg[1]
                    wl_b = [w for w in wl if w[0] is not sF1MM]
                    waits("scalar", wl_b)
                    op("scalar", lambda e, t=t, bb=bias_p: e.sign(
                        out=qsbt[t % 2][:], in_=wsl[t % 3][:], bias=bb[:]), sm=sSGN)
                    waits("scalar", [w for w in wl if w[0] is sF1MM])
                    op("scalar", lambda e, t=t, bb=bias_n: e.sign(
                        out=q12[t % 2][:], in_=wsl[t % 3][:], bias=bb[:]), sm=sSGN)
                    wl = [(sSGN, bSGN + 2 * t + 2)]
                    if t == 0:
                        wl.append((sS, S_FULL))
                        wl.append((sVT, VT_FULL))
                    op("vector", lambda e, t=t: e.tensor_tensor(
                        out=q12[t % 2][:], in0=q12[t % 2][:], in1=qsbt[t % 2][:],
                        op=Alu.add), sm=sQA, wl=wl)
                wl = [(sQA, bQA + 2 * j + 1), (sSIG, bSIG + j - 1)]
                if j == 0:
                    wl.append((sWEV, WEV_FULL))
                waits("tensor", wl)
                for i in range(KD):
                    for n0 in range(0, TC, NW512):
                        op("tensor", lambda e, j=j, i=i, n0=n0: e.matmul(
                            out=px1[j % 2][:, n0:n0 + NW512],
                            lhsT=q12[0][:, i * 128:(i + 1) * 128],
                            rhs=xn[:, i, n0:n0 + NW512],
                            start=(i == 0), stop=(i == KD - 1)),
                           sm=(sF1MM if (i == KD - 1 and n0 + NW512 >= TC) else None))
                waits("tensor", [(sQA, bQA + 2 * j + 2), (sHID, bHID + j - 1)])
                for i in range(KD):
                    for n0 in range(0, TC, NW512):
                        op("tensor", lambda e, j=j, i=i, n0=n0: e.matmul(
                            out=px2[j % 2][:, n0:n0 + NW512],
                            lhsT=q12[1][:, i * 128:(i + 1) * 128],
                            rhs=xn[:, i, n0:n0 + NW512],
                            start=(i == 0), stop=(i == KD - 1)),
                           sm=(sF1MM if (i == KD - 1 and n0 + NW512 >= TC) else None))
                if j >= 1:
                    f1_epilogue(j - 1)
            f1_epilogue(KH - 1)
            F1MM_FULL = sF1MM.v
            SIG_FULL = sSIG.v
            HID_FULL = sHID.v
            QA_FULL = sQA.v
            SGN_FULL = sSGN.v

        with contextlib.ExitStack() as ps:
            px3 = [ps.enter_context(nc.psum_tensor(f"px3_{c}_{i}", [128, TC], F32)) for i in range(2)]
            bSGN3, bQA3, bF2MM, bSILU, bRES = \
                sSGN3.v, sQA3.v, sF2MM.v, sSILU.v, sRES.v
            w3v = {}
            xv = {}
            xvh = {}
            last_out = {0: sOUTs[0].v, 1: sOUTs[1].v}

            def f2_epilogue(j):
                wl = [(sF2MM, bF2MM + NQT * (j + 1)), (sRES, bRES + 4 * j - 5)]
                op("scalar", lambda e, j=j: e.activation(
                    out=gsb[j % 2][:], in_=px3[j % 2][:], func=Act.Silu, scale=0.25),
                   sm=sSILU, wl=wl)
                for hh in range(2):
                    wlm = [(sSILU, bSILU + j + 1), (sOUTs[hh], last_out[hh]),
                           (sWEV, WEV_FULL)]
                    if j == 0:
                        wlm.append((sSGN, SGN_FULL))
                    op("vector", lambda e, j=j, hh=hh: e.tensor_tensor(
                        out=ostg[hh][:], in0=s_sb[:, j, hh * half:(hh + 1) * half],
                        in1=gsb[j % 2][:, hh * half:(hh + 1) * half], op=Alu.mult),
                       sm=sRES, wl=wlm)
                    op("vector", lambda e, j=j, hh=hh: e.tensor_tensor(
                        out=ostg[hh][:], in0=ostg[hh][:], in1=xstg[hh][:], op=Alu.add),
                       sm=sRES, wl=[(sXLs[hh], xvh[(j, hh)])], drain=True)
                    last_out[hh] = dma(
                        "gpsimd", d_out[j, :, c0 + hh * half:c0 + (hh + 1) * half],
                        ostg[hh][:], sOUTs[hh], wl=[(sRES, bRES + 4 * j + 2 * hh + 2)])

            for j in range(KD):
                for qt in range(NQT):
                    t = j * NQT + qt
                    qe = "sync" if t % 2 == 0 else "scalar"
                    wl = [(sSGN3, bSGN3 + 2 * t - 6)]
                    if t <= 3:
                        wl = [(sF1MM, F1MM_FULL)]
                    w3v[t] = dma(qe, w3sl[t % 4][:], d_w3[j, :, qt * W3C:(qt + 1) * W3C],
                                 sW3S[t % 4], wl=wl)
                for hh in range(2):
                    wl = [(sRES, bRES + 4 * (j - 1) + 2 * hh + 2)]
                    if j == 0:
                        wl = [(sWMM, WMM_FULL), (sSGN, SGN_FULL)] + \
                             [(s, s.v) for s in sPCp]
                    xvh[(j, hh)] = dma("gpsimd", xstg[hh][:],
                                       d_x[j, :, c0 + hh * half:c0 + (hh + 1) * half],
                                       sXLs[hh], wl=wl)
                for qt in range(NQT):
                    t = j * NQT + qt
                    ld = (sW3S[t % 4], w3v[t])
                    wl = [ld, (sF2MM, bF2MM + t - 2), (sQA3, bQA3 + t - 1)]
                    if t == 0:
                        wl.append((sBI, BI_FULL[0]))
                        wl.append((sQA, QA_FULL))
                    waits("scalar", wl)
                    op("scalar", lambda e, t=t: e.sign(
                        out=q3sl[t % 3][:], in_=w3sl[t % 4][:], bias=bneg[2][:]), sm=sSGN3)
                    op("scalar", lambda e, t=t: e.sign(
                        out=q3sbt[t % 2][:], in_=w3sl[t % 4][:], bias=bpos[2][:]), sm=sSGN3)
                    op("vector", lambda e, t=t: e.tensor_tensor(
                        out=q3sl[t % 3][:], in0=q3sl[t % 3][:], in1=q3sbt[t % 2][:],
                        op=Alu.add), sm=sQA3, wl=[(sSGN3, bSGN3 + 2 * t + 2)])
                wl0 = [(sSILU, bSILU + j - 1)]
                if j == 0:
                    wl0.append((sSIG, SIG_FULL))
                    wl0.append((sHID, HID_FULL))
                waits("tensor", wl0)
                for i in range(KH):
                    qi = i // KHQ
                    t = j * NQT + qi
                    if i % KHQ == 0:
                        waits("tensor", [(sQA3, bQA3 + t + 1)])
                    for n0 in range(0, TC, NW512):
                        op("tensor", lambda e, j=j, i=i, t=t, n0=n0: e.matmul(
                            out=px3[j % 2][:, n0:n0 + NW512],
                            lhsT=q3sl[t % 3][:, (i % KHQ) * 128:(i % KHQ + 1) * 128],
                            rhs=hid[:, i, n0:n0 + NW512],
                            start=(i == 0), stop=(i == KH - 1)),
                           sm=(sF2MM if ((i + 1) % KHQ == 0 and n0 + NW512 >= TC) else None))
                if j >= 1:
                    f2_epilogue(j - 1)
            f2_epilogue(KD - 1)
            prev = dict(SGN3=sSGN3.v, RES=sRES.v, OUT0=sOUTs[0].v, OUT1=sOUTs[1].v,
                        QA3=sQA3.v, QA=sQA.v, WMM=WMM_FULL, SILU=sSILU.v, AOE=AOE_FULL,
                        F1MM=F1MM_FULL)

    waits("gpsimd", [(sOUTs[0], sOUTs[0].v), (sOUTs[1], sOUTs[1].v)])

    with nc.Block() as block:
        @block.sync
        def _(eng):
            for fn in prog["sync"]:
                fn(eng)

        @block.tensor
        def _(eng):
            for fn in prog["tensor"]:
                fn(eng)

        @block.vector
        def _(eng):
            for fn in prog["vector"]:
                fn(eng)

        @block.scalar
        def _(eng):
            for fn in prog["scalar"]:
                fn(eng)

        @block.gpsimd
        def _(eng):
            for fn in prog["gpsimd"]:
                fn(eng)

    stack.close()
    return nc


def tile_w(W):
    O, I = W.shape
    KI, KO = I // 128, O // 128
    A = np.ascontiguousarray(np.asarray(W, dtype=np.float32).T)
    T = A.reshape(KI, 128, KO, 128).transpose(2, 1, 0, 3).reshape(KO, 128, KI * 128)
    return np.ascontiguousarray(T)


def host_inputs(P, x, Wq, Wk, Wv, Wo, w1, w2, w3):
    D, NH, T_core = P["D"], P["NH"], P["T_core"]
    KD = D // 128
    G = 128 // NH
    n_cores = P["n_cores"]
    tokens = np.ascontiguousarray(np.asarray(x, dtype=np.float32).reshape(-1, D))
    assert tokens.shape[0] == n_cores * T_core
    wq_t, wk_t, wv_t, wo_t = tile_w(Wq), tile_w(Wk), tile_w(Wv), tile_w(Wo)
    w1_t, w2_t, w3_t = tile_w(w1), tile_w(w2), tile_w(w3)
    idx = np.arange(128)
    mask = np.where((idx[:, None] % G) == (idx[None, :] % G), 0.0, NEG).astype(np.float32)
    idm = np.eye(128).astype(BF)
    onesb = np.ones((128, 1)).astype(BF)
    onescf = np.ones((128, 1), dtype=np.float32)
    onesr = np.ones((1, 128), dtype=np.float32)
    in_maps = []
    for cix in range(n_cores):
        shard = tokens[cix * T_core:(cix + 1) * T_core]
        x_fm = np.ascontiguousarray(shard.T).reshape(KD, 128, T_core)
        in_maps.append({"x": x_fm, "wq": wq_t, "wk": wk_t, "wv": wv_t, "wo": wo_t,
                        "w1": w1_t, "w2": w2_t, "w3": w3_t, "mask": mask, "idm": idm,
                        "onesb": onesb, "onescf": onescf, "onesr": onesr})
    return in_maps


def assemble_out(P, shards, out_shape):
    D, T_core = P["D"], P["T_core"]
    toks = []
    for r in shards:
        o = np.asarray(r["out"], dtype=np.float32).reshape(D, T_core)
        toks.append(o.T)
    return np.ascontiguousarray(np.concatenate(toks, axis=0).reshape(out_shape))


def kernel(L0_attn_x, L0_attn_Wq, L0_attn_Wk, L0_attn_Wv, L0_attn_Wo,
           L0_ffn_w1, L0_ffn_w2, L0_ffn_w3):
    P = FULL
    in_maps = host_inputs(P, L0_attn_x, L0_attn_Wq, L0_attn_Wk, L0_attn_Wv,
                          L0_attn_Wo, L0_ffn_w1, L0_ffn_w2, L0_ffn_w3)
    nc = build_program(P)
    res = run_bass_kernel_spmd(nc, in_maps, core_ids=list(range(P["n_cores"])))
    return assemble_out(P, res.results, np.asarray(L0_attn_x).shape)


# version 17
# speedup vs baseline: 1.1809x; 1.1809x over previous
import numpy as np
import ml_dtypes
import contextlib
import concourse.bass as bass
import concourse.mybir as mybir
from concourse.bass_utils import run_bass_kernel_spmd

F32 = mybir.dt.float32
BF16 = mybir.dt.bfloat16
BF = ml_dtypes.bfloat16
Alu = mybir.AluOpType
Act = mybir.ActivationFunctionType
AX = mybir.AxisListType.X

FULL = dict(D=2048, H=5632, NH=16, T_core=2048, TC=1024, n_cores=8)
NEG = -1.0e9
EPS_Q = 1e-8
EPS_LN = 1e-5


class Sem:
    def __init__(self, stack, nc, name):
        self.h = stack.enter_context(nc.semaphore(name))
        self.v = 0


def build_program(P):
    D, H, NH, T_core, TC = P["D"], P["H"], P["NH"], P["T_core"], P["TC"]
    KD, KH = D // 128, H // 128
    G = 128 // NH
    NG = TC // G
    NCH = T_core // TC
    half = TC // 2
    NQT = 4
    KHQ = KH // NQT
    W3C = H // NQT
    NW = float(H * D)
    NW512 = min(512, TC)
    assert D % 128 == 0 and H % 128 == 0 and KH % NQT == 0
    assert D // NH == 128, "head dim must be 128"
    assert TC % G == 0 and TC % 2 == 0 and T_core % TC == 0

    nc = bass.Bass("TRN2", target_bir_lowering=False)

    d_x = nc.dram_tensor("x", [KD, 128, T_core], F32, kind="ExternalInput").ap()
    d_wq = nc.dram_tensor("wq", [KD, 128, D], F32, kind="ExternalInput").ap()
    d_wk = nc.dram_tensor("wk", [KD, 128, D], F32, kind="ExternalInput").ap()
    d_wv = nc.dram_tensor("wv", [KD, 128, D], F32, kind="ExternalInput").ap()
    d_wo = nc.dram_tensor("wo", [KD, 128, D], F32, kind="ExternalInput").ap()
    d_w1 = nc.dram_tensor("w1", [KH, 128, D], F32, kind="ExternalInput").ap()
    d_w2 = nc.dram_tensor("w2", [KH, 128, D], F32, kind="ExternalInput").ap()
    d_w3 = nc.dram_tensor("w3", [KD, 128, H], F32, kind="ExternalInput").ap()
    d_mask = nc.dram_tensor("mask", [128, 128], F32, kind="ExternalInput").ap()
    d_id = nc.dram_tensor("idm", [128, 128], BF16, kind="ExternalInput").ap()
    d_onesb = nc.dram_tensor("onesb", [128, 1], BF16, kind="ExternalInput").ap()
    d_onescf = nc.dram_tensor("onescf", [128, 1], F32, kind="ExternalInput").ap()
    d_onesr = nc.dram_tensor("onesr", [1, 128], F32, kind="ExternalInput").ap()
    d_out = nc.dram_tensor("out", [KD, 128, T_core], F32, kind="ExternalOutput").ap()

    TCB = TC * 2
    qsz = NG * 256
    xn_off = 0
    s_off = xn_off + KD * TCB
    q_off = s_off + KD * TCB
    k_off = q_off + qsz
    v_off = k_off + qsz
    ao_off = v_off + qsz
    hid_off = q_off
    hid_sz = KH * TCB
    assert hid_sz <= 4 * qsz
    wstr_off = hid_off + hid_sz
    wstr_sz = 3 * D * 4
    qstr_off = wstr_off + wstr_sz
    qstr_sz = max(2 * D * 2, 3 * W3C * 2)
    after = max(ao_off + qsz, qstr_off + qstr_sz)
    sqg_off = after
    sqg_sz = 2 * TCB
    attws_off = sqg_off
    attws_sz = 2 * (512 + 4 * 256) + 4 * 32
    misc_off = sqg_off + max(sqg_sz, attws_sz, 4096)
    misc_sz = 512 + 256 + 64 + 64 + 512 + 6 * 32 + 6 * 32 + \
        4 * (KH * 2 + NQT * KD) + 3 * 32 + 512
    qsb_off = misc_off + misc_sz
    qsb_sz = max(2 * D * 2, 2 * W3C * 2, 4 * half * 4)
    total = qsb_off + qsb_sz
    stg_sz = 4 * W3C * 4
    if stg_sz <= KD * TCB:
        w3sl_off = xn_off
    else:
        w3sl_off = total
        total += stg_sz
    xstg_off = ao_off
    ostg_off = xstg_off + 2 * half * 4
    assert ostg_off + 2 * half * 4 <= qstr_off or qstr_off < ao_off
    if 4 * D * 4 <= KD * TCB:
        gam1_off = s_off
        gam2_off = s_off + 2 * D * 4
    else:
        gam1_off = total
        gam2_off = total + 2 * D * 4
        total += 4 * D * 4

    arena = nc.alloc_sbuf_tensor("arena", [128, total], mybir.dt.uint8)
    A0 = nc.lookup_mloc(arena).addr
    assert A0 + total <= nc.sbuf_top, f"sbuf overflow: {A0}+{total} > {nc.sbuf_top}"

    def at(name, off, shape, dtype):
        return nc.alloc_sbuf_tensor_at(name, shape, dtype, offset=A0 + off)

    xn = at("xn", xn_off, [128, KD, TC], BF16)
    s_sb = at("s_sb", s_off, [128, KD, TC], BF16)
    q_sb = at("q_sb", q_off, [128, NG, 128], BF16)
    k_sb = at("k_sb", k_off, [128, NG, 128], BF16)
    v_sb = at("v_sb", v_off, [128, NG, 128], BF16)
    ao_sb = at("ao_sb", ao_off, [128, NH, TC], BF16)
    hid = at("hid", hid_off, [128, KH, TC], BF16)
    wsl = [at(f"wsl{i}", wstr_off + i * D * 4, [128, D], F32) for i in range(3)]
    q12 = [at(f"q12_{i}", qstr_off + i * D * 2, [128, D], BF16) for i in range(2)]
    q3sl = [at(f"q3_{i}", qstr_off + i * W3C * 2, [128, W3C], BF16) for i in range(3)]
    w3sl = [at(f"w3sl{i}", w3sl_off + i * W3C * 4, [128, W3C], F32) for i in range(4)]
    NSPL_MAX = min(5, (4 * W3C * 4) // (TC * 4))
    xstg_f = [at(f"xstgf{i}", w3sl_off + i * TC * 4, [128, TC], F32)
              for i in range(max(NSPL_MAX, 1))]
    xstg = [at(f"xstg{i}", xstg_off + i * half * 4, [128, half], F32) for i in range(2)]
    ostg = [at(f"ostg{i}", ostg_off + i * half * 4, [128, half], F32) for i in range(2)]
    sqb = at("sqb", sqg_off, [128, TC], BF16)
    sig = [at(f"sig{i}", sqg_off + i * TCB, [128, TC], BF16) for i in range(2)]
    gsb = [at(f"gsb{i}", sqg_off + i * TCB, [128, TC], BF16) for i in range(2)]
    tmpw = [at(f"tmpw{i}", attws_off + i * 512, [128, 128], F32) for i in range(2)]
    Ab = [at(f"Ab{i}", attws_off + 1024 + i * 256, [128, 128], BF16) for i in range(2)]
    An = [at(f"An{i}", attws_off + 1536 + i * 256, [128, 128], BF16) for i in range(2)]
    ATs = [at(f"ATs{i}", attws_off + 2048 + i * 256, [128, 128], BF16) for i in range(2)]
    VTs = [at(f"VTs{i}", attws_off + 2560 + i * 256, [128, 128], BF16) for i in range(2)]
    rs = [at(f"rs{i}", attws_off + 3072 + i * 32, [128, 1], F32) for i in range(2)]
    rr = [at(f"rr{i}", attws_off + 3136 + i * 32, [128, 1], F32) for i in range(2)]
    mo = misc_off
    mask_sb = at("mask_sb", mo, [128, 128], F32); mo += 512
    id_sb = at("id_sb", mo, [128, 128], BF16); mo += 256
    onesb_sb = at("onesb_sb", mo, [128, 1], BF16); mo += 64
    onescf_sb = at("onescf_sb", mo, [128, 1], F32); mo += 64
    onesr_sb = at("onesr_sb", mo, [1, 128], F32); mo += 512
    bpos = []
    bneg = []
    for m in range(3):
        bpos.append(at(f"bpos{m}", mo, [128, 1], F32)); mo += 32
        bneg.append(at(f"bneg{m}", mo, [128, 1], F32)); mo += 32
    def _al(v):
        return (v + 31) & ~31
    gcols1 = at("gcols1", mo, [128, KH], F32); mo = _al(mo + KH * 4)
    gcols2 = at("gcols2", mo, [128, KH], F32); mo = _al(mo + KH * 4)
    gcols3 = at("gcols3", mo, [128, KD], F32); mo = _al(mo + KD * 4)
    gtmp = at("gtmp", mo, [128, 16], F32); mo = _al(mo + 64)
    g128 = []
    for m in range(3):
        g128.append(at(f"g128_{m}", mo, [128, 1], F32)); mo += 32
    gbv = []
    for m in range(3):
        gbv.append(at(f"gbv{m}", mo, [1, 1], F32)); mo += 32
    assert mo <= qsb_off
    qsbt = [at(f"qsbt{i}", qsb_off + i * D * 2, [128, D], BF16) for i in range(2)]
    q3sbt = [at(f"q3sbt{i}", qsb_off + i * W3C * 2, [128, W3C], BF16) for i in range(2)]
    pcst = [at(f"pcst{i}", ao_off + i * D * 2, [128, D], BF16) for i in range(4)]
    wcstw = [at(f"wcstw{i}", k_off + i * D * 2, [128, D], BF16) for i in range(4)]
    assert 4 * D * 2 <= qsz
    stv = [at(f"stv{i}", qsb_off + i * half * 4, [1, half], F32) for i in range(4)]
    st_negm, st_esq, st_tbuf, st_a = stv
    NR3 = min(16, qsz // 2048)
    gpcw = [at(f"gpcw{i}", q_off + i * 2048, [128, 512], F32) for i in range(NR3)]
    gs1 = [at(f"gs1_{i}", gam1_off + i * D * 4, [128, D], F32) for i in range(2)]
    gs2 = [at(f"gs2_{i}", gam2_off + i * D * 4, [128, D], F32) for i in range(2)]

    stack = contextlib.ExitStack()
    prog = {"sync": [], "tensor": [], "vector": [], "scalar": [], "gpsimd": []}

    def mksem(name):
        return Sem(stack, nc, name)

    def emit(e, fn):
        prog[e].append(fn)

    def waits(e, wl):
        for (sm, val) in wl:
            if val > 0:
                emit(e, lambda eng, sm=sm, val=val: eng.wait_ge(sm.h, val))

    def dma(e, out, in_, sm, wl=()):
        waits(e, wl)
        sm.v += 16
        emit(e, lambda eng, out=out, in_=in_, sm=sm:
             eng.dma_start(out=out, in_=in_).then_inc(sm.h, 16))
        return sm.v

    def op(e, fn, sm=None, wl=(), drain=False):
        if drain:
            emit(e, lambda eng: eng.drain())
        waits(e, wl)
        if sm is not None:
            sm.v += 1
            emit(e, lambda eng, fn=fn, sm=sm: fn(eng).then_inc(sm.h, 1))
        else:
            emit(e, lambda eng, fn=fn: fn(eng))

    sGPW = [mksem(f"gpw{i}") for i in range(NR3)]
    sGS1 = [mksem(f"g1s{i}") for i in range(2)]
    sGS2 = [mksem(f"g2s{i}") for i in range(2)]
    sGRU = mksem("gru")
    sGR = mksem("gr")
    sBI = mksem("bi")
    sLXt = [mksem(f"lxt{i}") for i in range(KD)]
    sLXf = [mksem(f"lxf{i}") for i in range(5)]
    sLXc = mksem("lxc")
    sSQ = mksem("lsq")
    sLST = mksem("lst")
    sLSV = mksem("lsv")
    sLBC = mksem("lbc")
    sLAP = mksem("lap")
    sPCp = [mksem(f"pcp{i}") for i in range(4)]
    sWCp = [mksem(f"wcp{i}") for i in range(4)]
    sPMM = mksem("pmm")
    sPEV = mksem("pev")
    sS, sSTT, sEXP, sNORM = mksem("as"), mksem("astt"), mksem("aexp"), mksem("anorm")
    sAT, sVT, sATC, sVTC = mksem("aat"), mksem("avt"), mksem("aatc"), mksem("avtc")
    sAV, sAOE = mksem("aav"), mksem("aaoe")
    sWMM = mksem("wmm")
    sWEV = mksem("wev")
    sWSL = [mksem(f"wsl_s{i}") for i in range(3)]
    sSGN = mksem("sgn")
    sQA = mksem("qa")
    sF1MM = mksem("f1mm")
    sSIG = mksem("sigm")
    sHID = mksem("hidm")
    sW3S = [mksem(f"w3s_s{i}") for i in range(4)]
    sSGN3 = mksem("sgn3")
    sQA3 = mksem("qa3")
    sF2MM = mksem("f2mm")
    sSILU = mksem("silu")
    sXLs = [mksem(f"xls{i}") for i in range(2)]
    sRES = mksem("res")
    sOUTs = [mksem(f"outs{i}") for i in range(2)]
    sCONST = mksem("cst")

    dma("sync", mask_sb[:], d_mask, sCONST)
    dma("sync", id_sb[:], d_id, sCONST)
    dma("sync", onesb_sb[:], d_onesb, sCONST)
    dma("sync", onescf_sb[:], d_onescf, sCONST)
    dma("sync", onesr_sb[:], d_onesr, sCONST)
    CONST_ALL = sCONST.v

    sGRv = mksem("grv")
    q1info = []
    for j in range(KH):
        q1info.append((d_w1[j], gs1[j % 2], sGS1[j % 2], gcols1[j:j + 1] if False else gcols1, j))
        q1info.append((d_w2[j], gs2[j % 2], sGS2[j % 2], gcols2, j))
    NQ1 = len(q1info)
    GRU_FULL = NQ1
    Q1 = {"pop": 0, "load": 0}
    Q1V = {}

    def _q1load(extra_wl):
        k = Q1["load"]
        if k >= NQ1:
            return
        ap, buf, sem, _, _ = q1info[k]
        wl = list(extra_wl) + ([(sGRU, k - 3)] if k - 3 > 0 else [])
        Q1V[k] = dma("sync", buf[:], ap, sem, wl=wl)
        Q1["load"] += 1

    def pop_q1(n, entry_wl=()):
        for _ in range(n):
            k = Q1["pop"]
            if k >= NQ1:
                return
            while Q1["load"] < min(k + 4, NQ1):
                _q1load(entry_wl)
            ap, buf, sem, gc, j = q1info[k]
            op("vector", lambda e, buf=buf, gc=gc, j=j: e.tensor_reduce(
                out=gc[:, j:j + 1], in_=buf[:], axis=AX, op=Alu.add,
                apply_absolute_value=True), sm=sGRU, wl=[(sem, Q1V[k])])
            Q1["pop"] += 1
    ginfo = []
    npp3 = H // 512
    for wd, gc, npp, nblk in ((d_w3, gcols3, npp3, KD),):
        for j in range(nblk):
            for p in range(npp):
                ginfo.append((wd[j, :, p * 512:(p + 1) * 512], p,
                              (gc, j, npp) if p == npp - 1 else None))
    NPIECES = len(ginfo)
    GST = {"pop": 0, "load": 0}
    GV = {}

    def _gload(extra_wl):
        k = GST["load"]
        if k >= NPIECES:
            return
        wl = list(extra_wl) + ([(sGRv, k - NR3 + 1)] if k - NR3 + 1 > 0 else [])
        GV[k] = dma("sync", gpcw[k % NR3][:], ginfo[k][0], sGPW[k % NR3], wl=wl)
        GST["load"] += 1

    def pop_gred(n, entry_wl=(), load_limit=None):
        lim = NPIECES if load_limit is None else min(load_limit, NPIECES)
        for _ in range(n):
            k = GST["pop"]
            if k >= NPIECES:
                return
            while GST["load"] < min(k + NR3 - 1, lim):
                _gload(entry_wl)
            _, p, bs = ginfo[k]
            op("vector", lambda e, k=k, p=p: e.tensor_reduce(
                out=gtmp[:, p:p + 1], in_=gpcw[k % NR3][:], axis=AX, op=Alu.add,
                apply_absolute_value=True), sm=sGRv,
               wl=[(sGPW[k % NR3], GV[k])], drain=(p == 0))
            if bs is not None:
                gc, j, npp = bs
                op("vector", lambda e, gc=gc, j=j, npp=npp: e.tensor_reduce(
                    out=gc[:, j:j + 1], in_=gtmp[:, 0:npp], axis=AX, op=Alu.add),
                   drain=True)
            GST["pop"] += 1

    def gamma_finals():
        op("vector", lambda e: e.tensor_reduce(out=g128[0][:], in_=gcols1[:], axis=AX,
                                               op=Alu.add), sm=sGR, drain=True,
           wl=[(s, s.v) for s in sGPW] + [(sGRv, sGRv.v),
               (sGS1[0], sGS1[0].v), (sGS1[1], sGS1[1].v),
               (sGS2[0], sGS2[0].v), (sGS2[1], sGS2[1].v),
               (sGRU, GRU_FULL)])
        op("vector", lambda e: e.tensor_reduce(out=g128[1][:], in_=gcols2[:], axis=AX,
                                               op=Alu.add), sm=sGR)
        op("vector", lambda e: e.tensor_reduce(out=g128[2][:], in_=gcols3[:], axis=AX,
                                               op=Alu.add), sm=sGR)
    GR_FULL = 3

    prev = dict(SGN3=0, RES=0, OUT0=0, OUT1=0, QA3=0, QA=0, WMM=0, SILU=0, AOE=0)
    BI_FULL = [0]

    for c in range(NCH):
        c0 = c * TC

        with contextlib.ExitStack() as ps:
            pssum = [ps.enter_context(nc.psum_tensor(f"pssum{c}_{h}", [1, half], F32)) for h in range(2)]
            pssq = [ps.enter_context(nc.psum_tensor(f"pssq{c}_{h}", [1, half], F32)) for h in range(2)]
            pa = [ps.enter_context(nc.psum_tensor(f"pa{c}_{h}", [128, half], F32)) for h in range(2)]
            pb = [ps.enter_context(nc.psum_tensor(f"pb{c}_{h}", [128, half], F32)) for h in range(2)]

            bSQ, bLST, bLSV, bLBC, bLAP = sSQ.v, sLST.v, sLSV.v, sLBC.v, sLAP.v
            lxv = {}
            n_w3_tiles = (4 * W3C * 4 + TCB - 1) // TCB if w3sl_off == xn_off else 0
            for ti in range(KD):
                wl = []
                if c > 0:
                    wl = [(sF1MM, prev["F1MM"])]
                    if ti < n_w3_tiles:
                        wl.append((sSGN3, prev["SGN3"]))
                lxv[ti] = dma("gpsimd", xn[:, ti, :], d_x[ti, :, c0:c0 + TC],
                              sLXt[ti], wl=wl)
            for ti in range(KD):
                if lxv[ti] is None:
                    wl = [(sLXc, bLXc + ti - (KD - NSPL) + 1), (sLST, bLST + ti)]
                else:
                    wl = [(sLXt[ti], lxv[ti]), (sLST, bLST + ti)]
                if ti == 0 and c > 0:
                    wl.append((sRES, prev["RES"]))
                    wl.append((sGR, GR_FULL))
                op("scalar", lambda e, ti=ti: e.square(out=sqb[:], in_=xn[:, ti, :]),
                   sm=sSQ, wl=wl)
                if lxv[ti] is None:
                    wl = [(sLXc, bLXc + ti - (KD - NSPL) + 1), (sSQ, bSQ + ti + 1)]
                else:
                    wl = [(sLXt[ti], lxv[ti]), (sSQ, bSQ + ti + 1)]
                if ti == 0:
                    if c == 0:
                        wl.append((sCONST, CONST_ALL))
                    else:
                        wl.append((sSILU, prev["SILU"]))
                        wl.append((sAOE, prev["AOE"]))
                        wl.append((sBI, BI_FULL[0]))
                waits("tensor", wl)
                for h in range(2):
                    op("tensor", lambda e, ti=ti, h=h: e.matmul(
                        out=pssum[h][:], lhsT=onesb_sb[:], rhs=xn[:, ti, h * half:(h + 1) * half],
                        start=(ti == 0), stop=(ti == KD - 1)))
                for h in range(2):
                    op("tensor", lambda e, ti=ti, h=h: e.matmul(
                        out=pssq[h][:], lhsT=onesb_sb[:], rhs=sqb[:, h * half:(h + 1) * half],
                        start=(ti == 0), stop=(ti == KD - 1)),
                       sm=(sLST if h == 1 else None))
            for h in range(2):
                waits("vector", [(sLST, bLST + KD)])
                op("vector", lambda e, h=h: e.tensor_scalar(
                    out=st_negm[:], in0=pssum[h][:], scalar1=-1.0 / D, scalar2=None, op0=Alu.mult),
                   drain=True)
                op("vector", lambda e, h=h: e.tensor_scalar(
                    out=st_esq[:], in0=pssq[h][:], scalar1=1.0 / D, scalar2=None, op0=Alu.mult),
                   drain=True)
                op("vector", lambda e: e.tensor_tensor(
                    out=st_tbuf[:], in0=st_negm[:], in1=st_negm[:], op=Alu.mult), drain=True)
                op("vector", lambda e: e.tensor_tensor(
                    out=st_esq[:], in0=st_esq[:], in1=st_tbuf[:], op=Alu.subtract), drain=True)
                op("vector", lambda e: e.tensor_scalar(
                    out=st_esq[:], in0=st_esq[:], scalar1=EPS_LN, scalar2=None, op0=Alu.add),
                   drain=True)
                op("vector", lambda e: e.reciprocal(out=st_tbuf[:], in_=st_esq[:]), sm=sLSV,
                   drain=True)
                op("scalar", lambda e: e.sqrt(out=st_a[:], in_=st_tbuf[:]),
                   sm=sLSV, wl=[(sLSV, bLSV + 3 * h + 1), (sLBC, bLBC + 2 * h)])
                op("vector", lambda e: e.tensor_tensor(
                    out=st_tbuf[:], in0=st_negm[:], in1=st_a[:], op=Alu.mult),
                   sm=sLSV, wl=[(sLSV, bLSV + 3 * h + 2)], drain=True)
                waits("tensor", [(sLSV, bLSV + 3 * h + 3)])
                op("tensor", lambda e, h=h: e.matmul(
                    out=pa[h][:], lhsT=onesr_sb[:], rhs=st_a[:], start=True, stop=True),
                   sm=sLBC)
                op("tensor", lambda e, h=h: e.matmul(
                    out=pb[h][:], lhsT=onesr_sb[:], rhs=st_tbuf[:], start=True, stop=True),
                   sm=sLBC)
                if h == 0:
                    waits("vector", [(sLBC, bLBC + 2)])
            for ti in range(KD):
                for h in range(2):
                    waits("vector", [(sLBC, bLBC + 2 * (h + 1))])
                    op("vector", lambda e, ti=ti, h=h: e.tensor_tensor(
                        out=sqb[:, 0:half], in0=xn[:, ti, h * half:(h + 1) * half],
                        in1=pa[h][:], op=Alu.mult), drain=True)
                    op("vector", lambda e, ti=ti, h=h: e.tensor_tensor(
                        out=xn[:, ti, h * half:(h + 1) * half], in0=sqb[:, 0:half],
                        in1=pb[h][:], op=Alu.add), sm=sLAP, drain=True)
            LAP_FULL = sLAP.v

        with contextlib.ExitStack() as ps:
            pacc = [ps.enter_context(nc.psum_tensor(f"pacc{c}_{i}", [128, TC], F32)) for i in range(2)]
            bPMM, bPEV = sPMM.v, sPEV.v
            LBC_FULL = sLBC.v
            b = 0
            for kind, wd in (("q", d_wq), ("k", d_wk), ("v", d_wv)):
                for j in range(KD):
                    wl = [(sPMM, bPMM + b - 3)]
                    if b <= 3:
                        wl.append((sLBC, LBC_FULL))
                        if c > 0:
                            wl.append((sQA3, prev["QA3"]))
                            wl.append((sQA, prev["QA"]))
                            wl.append((sRES, prev["RES"]))
                            wl.append((sOUTs[0], prev["OUT0"]))
                            wl.append((sOUTs[1], prev["OUT1"]))
                            wl.append((sAOE, prev["AOE"]))
                    vc = dma("gpsimd", pcst[b % 4][:], wd[j], sPCp[b % 4], wl=wl)
                    wl = [(sPCp[b % 4], vc), (sPEV, bPEV + b - 1)]
                    if b == 0:
                        wl.append((sLAP, LAP_FULL))
                    waits("tensor", wl)
                    for i in range(KD):
                        for n0 in range(0, TC, NW512):
                            op("tensor", lambda e, b=b, i=i, n0=n0: e.matmul(
                                out=pacc[b % 2][:, n0:n0 + NW512],
                                lhsT=pcst[b % 4][:, i * 128:(i + 1) * 128],
                                rhs=xn[:, i, n0:n0 + NW512],
                                start=(i == 0), stop=(i == KD - 1)),
                               sm=(sPMM if (i == KD - 1 and n0 + NW512 >= TC) else None))
                    dst = {"q": q_sb, "k": k_sb, "v": v_sb}[kind]
                    op("vector", lambda e, b=b, j=j, dst=dst: e.tensor_copy(
                        out=dst[:, :, j * G:(j + 1) * G],
                        in_=pacc[b % 2][:].rearrange("p (g t) -> p g t", t=G)),
                       sm=sPEV, wl=[(sPMM, bPMM + b + 1)])
                    if c == 0:
                        pop_q1(2)
                    b += 1
            PMM_FULL = sPMM.v

        with contextlib.ExitStack() as ps:
            pS = [ps.enter_context(nc.psum_tensor(f"pS{c}_{i}", [128, 128], F32)) for i in range(2)]
            pAT = [ps.enter_context(nc.psum_tensor(f"pAT{c}_{i}", [128, 128], BF16)) for i in range(2)]
            pVT = [ps.enter_context(nc.psum_tensor(f"pVT{c}_{i}", [128, 128], BF16)) for i in range(2)]
            pAV = [ps.enter_context(nc.psum_tensor(f"pAV{c}_{i}", [128, 128], F32)) for i in range(2)]
            bS, bSTT, bEXP, bNORM = sS.v, sSTT.v, sEXP.v, sNORM.v
            bAT, bVT, bATC, bVTC = sAT.v, sVT.v, sATC.v, sVTC.v
            bAV, bAOE = sAV.v, sAOE.v
            PEV_FULL = sPEV.v
            ISQ = float(1.0 / np.sqrt(128.0))

            for g in range(NG + 7):
                if g < NG:
                    gg = g
                    wl = [(sSTT, bSTT + gg - 1)]
                    if gg == 0:
                        wl.append((sPEV, PEV_FULL))
                    op("tensor", lambda e, gg=gg: e.matmul(
                        out=pS[gg % 2][:], lhsT=q_sb[:, gg, :], rhs=k_sb[:, gg, :],
                        start=True, stop=True), sm=sS, wl=wl)
                if 0 <= g - 1 < NG:
                    gg = g - 1
                    wl = [(sS, bS + gg + 1), (sEXP, bEXP + gg - 1)]
                    if gg == 0 and c == 0:
                        wl.append((sCONST, CONST_ALL))
                        wl.append((sGRv, sGRv.v))
                    op("vector", lambda e, gg=gg: e.scalar_tensor_tensor(
                        out=tmpw[gg % 2][:], in0=pS[gg % 2][:], scalar=ISQ,
                        in1=mask_sb[:], op0=Alu.mult, op1=Alu.add),
                       sm=sSTT, wl=wl)
                if 0 <= g - 2 < NG:
                    gg = g - 2
                    wl = [(sSTT, bSTT + gg + 1), (sNORM, bNORM + gg - 1)]
                    if gg == 0 and c == 0:
                        wl.append((sGRv, sGRv.v))
                    op("scalar", lambda e, gg=gg: e.activation(
                        out=Ab[gg % 2][:], in_=tmpw[gg % 2][:], func=Act.Exp,
                        accum_out=rs[gg % 2][:]),
                       sm=sEXP, wl=wl)
                if 0 <= g - 3 < NG:
                    gg = g - 3
                    waits("vector", [(sEXP, bEXP + gg + 1), (sAT, bAT + gg - 1)])
                    op("vector", lambda e, gg=gg: e.reciprocal(out=rr[gg % 2][:], in_=rs[gg % 2][:]))
                    op("vector", lambda e, gg=gg: e.tensor_scalar(
                        out=An[gg % 2][:], in0=Ab[gg % 2][:], scalar1=rr[gg % 2][:],
                        scalar2=None, op0=Alu.mult), sm=sNORM, drain=True)
                if 0 <= g - 4 < NG:
                    gg = g - 4
                    op("tensor", lambda e, gg=gg: e.transpose(
                        out=pAT[gg % 2][:], in_=An[gg % 2][:], identity=id_sb[:]),
                       sm=sAT, wl=[(sNORM, bNORM + gg + 1), (sATC, bATC + gg - 1)])
                    op("tensor", lambda e, gg=gg: e.transpose(
                        out=pVT[gg % 2][:], in_=v_sb[:, gg, :], identity=id_sb[:]),
                       sm=sVT, wl=[(sVTC, bVTC + gg - 1)])
                if 0 <= g - 5 < NG:
                    gg = g - 5
                    op("scalar", lambda e, gg=gg: e.copy(out=ATs[gg % 2][:], in_=pAT[gg % 2][:]),
                       sm=sATC, wl=[(sAT, bAT + gg + 1), (sAV, bAV + gg - 1)])
                    op("vector", lambda e, gg=gg: e.tensor_copy(out=VTs[gg % 2][:], in_=pVT[gg % 2][:]),
                       sm=sVTC, wl=[(sVT, bVT + gg + 1), (sAV, bAV + gg - 1)])
                if 0 <= g - 6 < NG:
                    gg = g - 6
                    op("tensor", lambda e, gg=gg: e.matmul(
                        out=pAV[gg % 2][:], lhsT=VTs[gg % 2][:], rhs=ATs[gg % 2][:],
                        start=True, stop=True),
                       sm=sAV, wl=[(sATC, bATC + gg + 1), (sVTC, bVTC + gg + 1),
                                   (sAOE, bAOE + gg - 1)])
                if 0 <= g - 7 < NG:
                    gg = g - 7
                    wl = [(sAV, bAV + gg + 1)]
                    if gg == 0 and c > 0:
                        wl.append((sWMM, prev["WMM"]))
                        wl.append((sRES, prev["RES"]))
                        wl.append((sOUTs[0], prev["OUT0"]))
                        wl.append((sOUTs[1], prev["OUT1"]))
                    op("scalar", lambda e, gg=gg: e.copy(
                        out=ao_sb[:, :, gg * G:(gg + 1) * G],
                        in_=pAV[gg % 2][:].rearrange("p (h t) -> p h t", t=G)),
                       sm=sAOE, wl=wl)
            S_FULL, VT_FULL, AOE_FULL = sS.v, sVT.v, sAOE.v
            EXP_FULL, NORM_FULL, AT_FULL, AV_FULL = sEXP.v, sNORM.v, sAT.v, sAV.v

        with contextlib.ExitStack() as ps:
            pacc = [ps.enter_context(nc.psum_tensor(f"wacc{c}_{i}", [128, TC], F32)) for i in range(2)]
            bWMM, bWEV = sWMM.v, sWEV.v
            for j in range(KD):
                wl = [(sWMM, bWMM + j - 3)]
                if j <= 3:
                    wl.append((sS, S_FULL))
                vc = dma("gpsimd", wcstw[j % 4][:], d_wo[j], sWCp[j % 4], wl=wl)
                wl = [(sWCp[j % 4], vc), (sWEV, bWEV + j - 1)]
                if j == 0:
                    wl.append((sAOE, AOE_FULL))
                waits("tensor", wl)
                for i in range(KD):
                    for n0 in range(0, TC, NW512):
                        op("tensor", lambda e, j=j, i=i, n0=n0: e.matmul(
                            out=pacc[j % 2][:, n0:n0 + NW512],
                            lhsT=wcstw[j % 4][:, i * 128:(i + 1) * 128],
                            rhs=ao_sb[:, i, n0:n0 + NW512],
                            start=(i == 0), stop=(i == KD - 1)),
                           sm=(sWMM if (i == KD - 1 and n0 + NW512 >= TC) else None))
                wl = [(sWMM, bWMM + j + 1)]
                if j == 0 and c == 0:
                    pop_q1(NQ1 + 1)
                    wl.append((sGRU, GRU_FULL))
                if j == 0 and c > 0:
                    wl.append((sOUTs[0], prev["OUT0"]))
                    wl.append((sOUTs[1], prev["OUT1"]))
                op("vector", lambda e, j=j: e.tensor_copy(
                    out=s_sb[:, j, :], in_=pacc[j % 2][:]), sm=sWEV, wl=wl)
                if c == 0:
                    pop_q1(NQ1 + 1)
                    aw = [(sEXP, EXP_FULL), (sNORM, NORM_FULL), (sAT, AT_FULL),
                          (sAV, AV_FULL), (sS, S_FULL)]
                    pop_gred(12 if j < KD - 1 else NPIECES + 1, entry_wl=aw)
                    if j == KD - 1:
                        gamma_finals()
            WMM_FULL = sWMM.v
            WEV_FULL = sWEV.v

            if c == 0:
                pgs = ps.enter_context(nc.psum_tensor("pgs", [1, 1], F32))
                pgb = ps.enter_context(nc.psum_tensor("pgb", [128, 1], F32))
                bBI = sBI.v
                for m in range(3):
                    op("tensor", lambda e, m=m: e.matmul(
                        out=pgs[:], lhsT=g128[m][:], rhs=onescf_sb[:], start=True, stop=True),
                       sm=sBI, wl=[(sGR, GR_FULL), (sBI, bBI + 4 * m)])
                    op("vector", lambda e, m=m: e.tensor_scalar(
                        out=gbv[m][:], in0=pgs[:], scalar1=0.5 / NW, scalar2=EPS_Q * 0.5,
                        op0=Alu.mult, op1=Alu.add), sm=sBI, wl=[(sBI, bBI + 4 * m + 1)])
                    op("tensor", lambda e, m=m: e.matmul(
                        out=pgb[:], lhsT=onesr_sb[:], rhs=gbv[m][:], start=True, stop=True),
                       sm=sBI, wl=[(sBI, bBI + 4 * m + 2)])
                    waits("vector", [(sBI, bBI + 4 * m + 3)])
                    op("vector", lambda e, m=m: e.tensor_copy(out=bpos[m][:], in_=pgb[:]))
                    op("vector", lambda e, m=m: e.tensor_scalar(
                        out=bneg[m][:], in0=bpos[m][:], scalar1=-1.0, scalar2=None,
                        op0=Alu.mult), sm=sBI, drain=True)
                BI_FULL[0] = sBI.v

        with contextlib.ExitStack() as ps:
            px1 = [ps.enter_context(nc.psum_tensor(f"px1_{c}_{i}", [128, TC], F32)) for i in range(2)]
            px2 = [ps.enter_context(nc.psum_tensor(f"px2_{c}_{i}", [128, TC], F32)) for i in range(2)]
            bSGN, bQA, bF1MM, bSIG, bHID = \
                sSGN.v, sQA.v, sF1MM.v, sSIG.v, sHID.v

            def f1_epilogue(j):
                wl = [(sF1MM, bF1MM + 2 * j + 1), (sHID, bHID + j - 1)]
                if j <= 1 and c == 0:
                    wl.append((sGR, GR_FULL))
                if j <= 1 and c > 0:
                    wl.append((sRES, prev["RES"]))
                op("scalar", lambda e, j=j: e.activation(
                    out=sig[j % 2][:], in_=px1[j % 2][:], func=Act.Sigmoid, scale=0.5),
                   sm=sSIG, wl=wl)
                op("vector", lambda e, j=j: e.tensor_tensor(
                    out=hid[:, j, :], in0=sig[j % 2][:], in1=px2[j % 2][:], op=Alu.mult),
                   sm=sHID, wl=[(sSIG, bSIG + j + 1), (sF1MM, bF1MM + 2 * j + 2)])

            for j in range(KH):
                wl1 = [(sSGN, bSGN + 4 * j - 4)]
                wl2 = [(sSGN, bSGN + 4 * j - 2)]
                if j == 0:
                    entry = [(sS, S_FULL), (sVT, VT_FULL), (sWMM, WMM_FULL),
                             (sGR, GR_FULL)]
                    if c > 0:
                        entry += [(sRES, prev["RES"]), (sOUTs[0], prev["OUT0"]),
                                  (sOUTs[1], prev["OUT1"])]
                    wl1 = entry
                    wl2 = entry
                vl1 = dma("sync", wsl[(2 * j) % 3][:], d_w1[j], sWSL[(2 * j) % 3], wl=wl1)
                vl2 = dma("scalar", wsl[(2 * j + 1) % 3][:], d_w2[j], sWSL[(2 * j + 1) % 3], wl=wl2)
                for t, ldw in ((2 * j, (sWSL[(2 * j) % 3], vl1)),
                               (2 * j + 1, (sWSL[(2 * j + 1) % 3], vl2))):
                    wl = [ldw, (sF1MM, bF1MM + t - 1), (sQA, bQA + t - 1)]
                    if t == 0:
                        wl.append((sBI, bBI + 8 if c == 0 else BI_FULL[0]))
                        wl.append((sWMM, WMM_FULL))
                        wl.append((sS, S_FULL))
                        wl.append((sVT, VT_FULL))
                    bias_p = bpos[0] if t % 2 == 0 else bpos[1]
                    bias_n = bneg[0] if t % 2 == 0 else bneg[1]
                    wl_b = [w for w in wl if w[0] is not sF1MM]
                    waits("scalar", wl_b)
                    op("scalar", lambda e, t=t, bb=bias_p: e.sign(
                        out=qsbt[t % 2][:], in_=wsl[t % 3][:], bias=bb[:]), sm=sSGN)
                    waits("scalar", [w for w in wl if w[0] is sF1MM])
                    op("scalar", lambda e, t=t, bb=bias_n: e.sign(
                        out=q12[t % 2][:], in_=wsl[t % 3][:], bias=bb[:]), sm=sSGN)
                    wl = [(sSGN, bSGN + 2 * t + 2)]
                    if t == 0:
                        wl.append((sS, S_FULL))
                        wl.append((sVT, VT_FULL))
                    op("vector", lambda e, t=t: e.tensor_tensor(
                        out=q12[t % 2][:], in0=q12[t % 2][:], in1=qsbt[t % 2][:],
                        op=Alu.add), sm=sQA, wl=wl)
                wl = [(sQA, bQA + 2 * j + 1), (sSIG, bSIG + j - 1)]
                if j == 0:
                    wl.append((sWEV, WEV_FULL))
                waits("tensor", wl)
                for i in range(KD):
                    for n0 in range(0, TC, NW512):
                        op("tensor", lambda e, j=j, i=i, n0=n0: e.matmul(
                            out=px1[j % 2][:, n0:n0 + NW512],
                            lhsT=q12[0][:, i * 128:(i + 1) * 128],
                            rhs=xn[:, i, n0:n0 + NW512],
                            start=(i == 0), stop=(i == KD - 1)),
                           sm=(sF1MM if (i == KD - 1 and n0 + NW512 >= TC) else None))
                waits("tensor", [(sQA, bQA + 2 * j + 2), (sHID, bHID + j - 1)])
                for i in range(KD):
                    for n0 in range(0, TC, NW512):
                        op("tensor", lambda e, j=j, i=i, n0=n0: e.matmul(
                            out=px2[j % 2][:, n0:n0 + NW512],
                            lhsT=q12[1][:, i * 128:(i + 1) * 128],
                            rhs=xn[:, i, n0:n0 + NW512],
                            start=(i == 0), stop=(i == KD - 1)),
                           sm=(sF1MM if (i == KD - 1 and n0 + NW512 >= TC) else None))
                if j >= 1:
                    f1_epilogue(j - 1)
            f1_epilogue(KH - 1)
            F1MM_FULL = sF1MM.v
            SIG_FULL = sSIG.v
            HID_FULL = sHID.v
            QA_FULL = sQA.v
            SGN_FULL = sSGN.v

        with contextlib.ExitStack() as ps:
            px3 = [ps.enter_context(nc.psum_tensor(f"px3_{c}_{i}", [128, TC], F32)) for i in range(2)]
            bSGN3, bQA3, bF2MM, bSILU, bRES = \
                sSGN3.v, sQA3.v, sF2MM.v, sSILU.v, sRES.v
            w3v = {}
            xv = {}
            xvh = {}
            last_out = {0: sOUTs[0].v, 1: sOUTs[1].v}

            def f2_epilogue(j):
                wl = [(sF2MM, bF2MM + NQT * (j + 1)), (sRES, bRES + 4 * j - 5)]
                op("scalar", lambda e, j=j: e.activation(
                    out=gsb[j % 2][:], in_=px3[j % 2][:], func=Act.Silu, scale=0.25),
                   sm=sSILU, wl=wl)
                for hh in range(2):
                    wlm = [(sSILU, bSILU + j + 1), (sOUTs[hh], last_out[hh]),
                           (sWEV, WEV_FULL)]
                    if j == 0:
                        wlm.append((sSGN, SGN_FULL))
                    op("vector", lambda e, j=j, hh=hh: e.tensor_tensor(
                        out=ostg[hh][:], in0=s_sb[:, j, hh * half:(hh + 1) * half],
                        in1=gsb[j % 2][:, hh * half:(hh + 1) * half], op=Alu.mult),
                       sm=sRES, wl=wlm)
                    op("vector", lambda e, j=j, hh=hh: e.tensor_tensor(
                        out=ostg[hh][:], in0=ostg[hh][:], in1=xstg[hh][:], op=Alu.add),
                       sm=sRES, wl=[(sXLs[hh], xvh[(j, hh)])], drain=True)
                    last_out[hh] = dma(
                        "gpsimd", d_out[j, :, c0 + hh * half:c0 + (hh + 1) * half],
                        ostg[hh][:], sOUTs[hh], wl=[(sRES, bRES + 4 * j + 2 * hh + 2)])

            for j in range(KD):
                for qt in range(NQT):
                    t = j * NQT + qt
                    qe = "sync" if t % 2 == 0 else "scalar"
                    wl = [(sSGN3, bSGN3 + 2 * t - 6)]
                    if t <= 3:
                        wl = [(sF1MM, F1MM_FULL)]
                    w3v[t] = dma(qe, w3sl[t % 4][:], d_w3[j, :, qt * W3C:(qt + 1) * W3C],
                                 sW3S[t % 4], wl=wl)
                for hh in range(2):
                    wl = [(sRES, bRES + 4 * (j - 1) + 2 * hh + 2)]
                    if j == 0:
                        wl = [(sWMM, WMM_FULL), (sSGN, SGN_FULL)] + \
                             [(s, s.v) for s in sPCp]
                    xvh[(j, hh)] = dma("gpsimd", xstg[hh][:],
                                       d_x[j, :, c0 + hh * half:c0 + (hh + 1) * half],
                                       sXLs[hh], wl=wl)
                for qt in range(NQT):
                    t = j * NQT + qt
                    ld = (sW3S[t % 4], w3v[t])
                    wl = [ld, (sF2MM, bF2MM + t - 2), (sQA3, bQA3 + t - 1)]
                    if t == 0:
                        wl.append((sBI, BI_FULL[0]))
                        wl.append((sQA, QA_FULL))
                    waits("scalar", wl)
                    op("scalar", lambda e, t=t: e.sign(
                        out=q3sl[t % 3][:], in_=w3sl[t % 4][:], bias=bneg[2][:]), sm=sSGN3)
                    op("scalar", lambda e, t=t: e.sign(
                        out=q3sbt[t % 2][:], in_=w3sl[t % 4][:], bias=bpos[2][:]), sm=sSGN3)
                    op("vector", lambda e, t=t: e.tensor_tensor(
                        out=q3sl[t % 3][:], in0=q3sl[t % 3][:], in1=q3sbt[t % 2][:],
                        op=Alu.add), sm=sQA3, wl=[(sSGN3, bSGN3 + 2 * t + 2)])
                wl0 = [(sSILU, bSILU + j - 1)]
                if j == 0:
                    wl0.append((sSIG, SIG_FULL))
                    wl0.append((sHID, HID_FULL))
                waits("tensor", wl0)
                for i in range(KH):
                    qi = i // KHQ
                    t = j * NQT + qi
                    if i % KHQ == 0:
                        waits("tensor", [(sQA3, bQA3 + t + 1)])
                    for n0 in range(0, TC, NW512):
                        op("tensor", lambda e, j=j, i=i, t=t, n0=n0: e.matmul(
                            out=px3[j % 2][:, n0:n0 + NW512],
                            lhsT=q3sl[t % 3][:, (i % KHQ) * 128:(i % KHQ + 1) * 128],
                            rhs=hid[:, i, n0:n0 + NW512],
                            start=(i == 0), stop=(i == KH - 1)),
                           sm=(sF2MM if ((i + 1) % KHQ == 0 and n0 + NW512 >= TC) else None))
                if j >= 1:
                    f2_epilogue(j - 1)
            f2_epilogue(KD - 1)
            prev = dict(SGN3=sSGN3.v, RES=sRES.v, OUT0=sOUTs[0].v, OUT1=sOUTs[1].v,
                        QA3=sQA3.v, QA=sQA.v, WMM=WMM_FULL, SILU=sSILU.v, AOE=AOE_FULL,
                        F1MM=F1MM_FULL)

    waits("gpsimd", [(sOUTs[0], sOUTs[0].v), (sOUTs[1], sOUTs[1].v)])

    with nc.Block() as block:
        @block.sync
        def _(eng):
            for fn in prog["sync"]:
                fn(eng)

        @block.tensor
        def _(eng):
            for fn in prog["tensor"]:
                fn(eng)

        @block.vector
        def _(eng):
            for fn in prog["vector"]:
                fn(eng)

        @block.scalar
        def _(eng):
            for fn in prog["scalar"]:
                fn(eng)

        @block.gpsimd
        def _(eng):
            for fn in prog["gpsimd"]:
                fn(eng)

    stack.close()
    return nc


def tile_w(W):
    O, I = W.shape
    KI, KO = I // 128, O // 128
    A = np.ascontiguousarray(np.asarray(W, dtype=np.float32).T)
    T = A.reshape(KI, 128, KO, 128).transpose(2, 1, 0, 3).reshape(KO, 128, KI * 128)
    return np.ascontiguousarray(T)


def host_inputs(P, x, Wq, Wk, Wv, Wo, w1, w2, w3):
    D, NH, T_core = P["D"], P["NH"], P["T_core"]
    KD = D // 128
    G = 128 // NH
    n_cores = P["n_cores"]
    tokens = np.ascontiguousarray(np.asarray(x, dtype=np.float32).reshape(-1, D))
    assert tokens.shape[0] == n_cores * T_core
    wq_t, wk_t, wv_t, wo_t = tile_w(Wq), tile_w(Wk), tile_w(Wv), tile_w(Wo)
    w1_t, w2_t, w3_t = tile_w(w1), tile_w(w2), tile_w(w3)
    idx = np.arange(128)
    mask = np.where((idx[:, None] % G) == (idx[None, :] % G), 0.0, NEG).astype(np.float32)
    idm = np.eye(128).astype(BF)
    onesb = np.ones((128, 1)).astype(BF)
    onescf = np.ones((128, 1), dtype=np.float32)
    onesr = np.ones((1, 128), dtype=np.float32)
    in_maps = []
    for cix in range(n_cores):
        shard = tokens[cix * T_core:(cix + 1) * T_core]
        x_fm = np.ascontiguousarray(shard.T).reshape(KD, 128, T_core)
        in_maps.append({"x": x_fm, "wq": wq_t, "wk": wk_t, "wv": wv_t, "wo": wo_t,
                        "w1": w1_t, "w2": w2_t, "w3": w3_t, "mask": mask, "idm": idm,
                        "onesb": onesb, "onescf": onescf, "onesr": onesr})
    return in_maps


def assemble_out(P, shards, out_shape):
    D, T_core = P["D"], P["T_core"]
    toks = []
    for r in shards:
        o = np.asarray(r["out"], dtype=np.float32).reshape(D, T_core)
        toks.append(o.T)
    return np.ascontiguousarray(np.concatenate(toks, axis=0).reshape(out_shape))


def kernel(L0_attn_x, L0_attn_Wq, L0_attn_Wk, L0_attn_Wv, L0_attn_Wo,
           L0_ffn_w1, L0_ffn_w2, L0_ffn_w3):
    P = FULL
    in_maps = host_inputs(P, L0_attn_x, L0_attn_Wq, L0_attn_Wk, L0_attn_Wv,
                          L0_attn_Wo, L0_ffn_w1, L0_ffn_w2, L0_ffn_w3)
    nc = build_program(P)
    res = run_bass_kernel_spmd(nc, in_maps, core_ids=list(range(P["n_cores"])))
    return assemble_out(P, res.results, np.asarray(L0_attn_x).shape)


# version 19
# speedup vs baseline: 1.1988x; 1.0152x over previous
import numpy as np
import ml_dtypes
import contextlib
import concourse.bass as bass
import concourse.mybir as mybir
from concourse.bass_utils import run_bass_kernel_spmd

F32 = mybir.dt.float32
BF16 = mybir.dt.bfloat16
BF = ml_dtypes.bfloat16
Alu = mybir.AluOpType
Act = mybir.ActivationFunctionType
AX = mybir.AxisListType.X

FULL = dict(D=2048, H=5632, NH=16, T_core=2048, TC=1024, n_cores=8)
NEG = -1.0e9
EPS_Q = 1e-8
EPS_LN = 1e-5


class Sem:
    def __init__(self, stack, nc, name):
        self.h = stack.enter_context(nc.semaphore(name))
        self.v = 0


def build_program(P):
    D, H, NH, T_core, TC = P["D"], P["H"], P["NH"], P["T_core"], P["TC"]
    KD, KH = D // 128, H // 128
    G = 128 // NH
    NG = TC // G
    NCH = T_core // TC
    half = TC // 2
    NQT = 4
    KHQ = KH // NQT
    W3C = H // NQT
    NW = float(H * D)
    NW512 = min(512, TC)
    assert D % 128 == 0 and H % 128 == 0 and KH % NQT == 0
    assert D // NH == 128, "head dim must be 128"
    assert TC % G == 0 and TC % 2 == 0 and T_core % TC == 0

    nc = bass.Bass("TRN2", target_bir_lowering=False)

    d_x = nc.dram_tensor("x", [KD, 128, T_core], F32, kind="ExternalInput").ap()
    d_wq = nc.dram_tensor("wq", [KD, 128, D], F32, kind="ExternalInput").ap()
    d_wk = nc.dram_tensor("wk", [KD, 128, D], F32, kind="ExternalInput").ap()
    d_wv = nc.dram_tensor("wv", [KD, 128, D], F32, kind="ExternalInput").ap()
    d_wo = nc.dram_tensor("wo", [KD, 128, D], F32, kind="ExternalInput").ap()
    d_w1 = nc.dram_tensor("w1", [KH, 128, D], F32, kind="ExternalInput").ap()
    d_w2 = nc.dram_tensor("w2", [KH, 128, D], F32, kind="ExternalInput").ap()
    d_w3 = nc.dram_tensor("w3", [KD, 128, H], F32, kind="ExternalInput").ap()
    d_mask = nc.dram_tensor("mask", [128, 128], F32, kind="ExternalInput").ap()
    d_id = nc.dram_tensor("idm", [128, 128], BF16, kind="ExternalInput").ap()
    d_onesb = nc.dram_tensor("onesb", [128, 1], BF16, kind="ExternalInput").ap()
    d_onescf = nc.dram_tensor("onescf", [128, 1], F32, kind="ExternalInput").ap()
    d_onesr = nc.dram_tensor("onesr", [1, 128], F32, kind="ExternalInput").ap()
    d_out = nc.dram_tensor("out", [KD, 128, T_core], F32, kind="ExternalOutput").ap()

    TCB = TC * 2
    qsz = NG * 256
    xn_off = 0
    s_off = xn_off + KD * TCB
    q_off = s_off + KD * TCB
    k_off = q_off + qsz
    v_off = k_off + qsz
    ao_off = v_off + qsz
    hid_off = q_off
    hid_sz = KH * TCB
    assert hid_sz <= 4 * qsz
    wstr_off = hid_off + hid_sz
    wstr_sz = 3 * D * 4
    qstr_off = wstr_off + wstr_sz
    qstr_sz = max(2 * D * 2, 3 * W3C * 2)
    after = max(ao_off + qsz, qstr_off + qstr_sz)
    sqg_off = after
    sqg_sz = 2 * TCB
    attws_off = sqg_off
    attws_sz = 2 * (512 + 4 * 256) + 4 * 32
    misc_off = sqg_off + max(sqg_sz, attws_sz, 4096)
    misc_sz = 512 + 256 + 64 + 64 + 512 + 6 * 32 + 6 * 32 + \
        4 * (KH * 2 + NQT * KD) + 3 * 32 + 512
    qsb_off = misc_off + misc_sz
    qsb_sz = max(2 * D * 2, 2 * W3C * 2, 4 * half * 4)
    total = qsb_off + qsb_sz
    stg_sz = 4 * W3C * 4
    if stg_sz <= KD * TCB:
        w3sl_off = xn_off
    else:
        w3sl_off = total
        total += stg_sz
    xstg_off = ao_off
    ostg_off = xstg_off + 2 * half * 4
    assert ostg_off + 2 * half * 4 <= qstr_off or qstr_off < ao_off
    if 4 * D * 4 <= KD * TCB:
        gam1_off = s_off
        gam2_off = s_off + 2 * D * 4
    else:
        gam1_off = total
        gam2_off = total + 2 * D * 4
        total += 4 * D * 4

    arena = nc.alloc_sbuf_tensor("arena", [128, total], mybir.dt.uint8)
    A0 = nc.lookup_mloc(arena).addr
    assert A0 + total <= nc.sbuf_top, f"sbuf overflow: {A0}+{total} > {nc.sbuf_top}"

    def at(name, off, shape, dtype):
        return nc.alloc_sbuf_tensor_at(name, shape, dtype, offset=A0 + off)

    xn = at("xn", xn_off, [128, KD, TC], BF16)
    s_sb = at("s_sb", s_off, [128, KD, TC], BF16)
    q_sb = at("q_sb", q_off, [128, NG, 128], BF16)
    k_sb = at("k_sb", k_off, [128, NG, 128], BF16)
    v_sb = at("v_sb", v_off, [128, NG, 128], BF16)
    ao_sb = at("ao_sb", ao_off, [128, NH, TC], BF16)
    hid = at("hid", hid_off, [128, KH, TC], BF16)
    wsl = [at(f"wsl{i}", wstr_off + i * D * 4, [128, D], F32) for i in range(3)]
    q12 = [at(f"q12_{i}", qstr_off + i * D * 2, [128, D], BF16) for i in range(2)]
    q3sl = [at(f"q3_{i}", qstr_off + i * W3C * 2, [128, W3C], BF16) for i in range(3)]
    w3sl = [at(f"w3sl{i}", w3sl_off + i * W3C * 4, [128, W3C], F32) for i in range(4)]
    NSPL_MAX = min(5, (4 * W3C * 4) // (TC * 4))
    xstg_f = [at(f"xstgf{i}", w3sl_off + i * TC * 4, [128, TC], F32)
              for i in range(max(NSPL_MAX, 1))]
    xstg = [at(f"xstg{i}", xstg_off + i * half * 4, [128, half], F32) for i in range(2)]
    ostg = [at(f"ostg{i}", ostg_off + i * half * 4, [128, half], F32) for i in range(2)]
    sqb = at("sqb", sqg_off, [128, TC], BF16)
    pa_sb = at("pa_sb", sqg_off + TCB, [128, half], BF16)
    pb_sb = at("pb_sb", sqg_off + TCB + half * 2, [128, half], BF16)
    assert TCB + half * 4 <= max(sqg_sz, 4096)
    sig = [at(f"sig{i}", sqg_off + i * TCB, [128, TC], BF16) for i in range(2)]
    gsb = [at(f"gsb{i}", sqg_off + i * TCB, [128, TC], BF16) for i in range(2)]
    tmpw = [at(f"tmpw{i}", attws_off + i * 512, [128, 128], F32) for i in range(2)]
    Ab = [at(f"Ab{i}", attws_off + 1024 + i * 256, [128, 128], BF16) for i in range(2)]
    An = [at(f"An{i}", attws_off + 1536 + i * 256, [128, 128], BF16) for i in range(2)]
    ATs = [at(f"ATs{i}", attws_off + 2048 + i * 256, [128, 128], BF16) for i in range(2)]
    VTs = [at(f"VTs{i}", attws_off + 2560 + i * 256, [128, 128], BF16) for i in range(2)]
    rs = [at(f"rs{i}", attws_off + 3072 + i * 32, [128, 1], F32) for i in range(2)]
    rr = [at(f"rr{i}", attws_off + 3136 + i * 32, [128, 1], F32) for i in range(2)]
    mo = misc_off
    mask_sb = at("mask_sb", mo, [128, 128], F32); mo += 512
    id_sb = at("id_sb", mo, [128, 128], BF16); mo += 256
    onesb_sb = at("onesb_sb", mo, [128, 1], BF16); mo += 64
    onescf_sb = at("onescf_sb", mo, [128, 1], F32); mo += 64
    onesr_sb = at("onesr_sb", mo, [1, 128], F32); mo += 512
    bpos = []
    bneg = []
    for m in range(3):
        bpos.append(at(f"bpos{m}", mo, [128, 1], F32)); mo += 32
        bneg.append(at(f"bneg{m}", mo, [128, 1], F32)); mo += 32
    def _al(v):
        return (v + 31) & ~31
    gcols1 = at("gcols1", mo, [128, KH], F32); mo = _al(mo + KH * 4)
    gcols2 = at("gcols2", mo, [128, KH], F32); mo = _al(mo + KH * 4)
    gcols3 = at("gcols3", mo, [128, KD], F32); mo = _al(mo + KD * 4)
    gtmp = at("gtmp", mo, [128, 16], F32); mo = _al(mo + 64)
    g128 = []
    for m in range(3):
        g128.append(at(f"g128_{m}", mo, [128, 1], F32)); mo += 32
    gbv = []
    for m in range(3):
        gbv.append(at(f"gbv{m}", mo, [1, 1], F32)); mo += 32
    assert mo <= qsb_off
    qsbt = [at(f"qsbt{i}", qsb_off + i * D * 2, [128, D], BF16) for i in range(2)]
    q3sbt = [at(f"q3sbt{i}", qsb_off + i * W3C * 2, [128, W3C], BF16) for i in range(2)]
    pcst = [at(f"pcst{i}", ao_off + i * D * 2, [128, D], BF16) for i in range(4)]
    wcstw = [at(f"wcstw{i}", k_off + i * D * 2, [128, D], BF16) for i in range(4)]
    assert 4 * D * 2 <= qsz
    stv = [at(f"stv{i}", qsb_off + i * half * 4, [1, half], F32) for i in range(4)]
    st_negm, st_esq, st_tbuf, st_a = stv
    NR3 = min(16, qsz // 2048)
    gpcw = [at(f"gpcw{i}", q_off + i * 2048, [128, 512], F32) for i in range(NR3)]
    gs1 = [at(f"gs1_{i}", gam1_off + i * D * 4, [128, D], F32) for i in range(2)]
    gs2 = [at(f"gs2_{i}", gam2_off + i * D * 4, [128, D], F32) for i in range(2)]

    stack = contextlib.ExitStack()
    prog = {"sync": [], "tensor": [], "vector": [], "scalar": [], "gpsimd": []}

    def mksem(name):
        return Sem(stack, nc, name)

    def emit(e, fn):
        prog[e].append(fn)

    def waits(e, wl):
        for (sm, val) in wl:
            if val > 0:
                emit(e, lambda eng, sm=sm, val=val: eng.wait_ge(sm.h, val))

    def dma(e, out, in_, sm, wl=()):
        waits(e, wl)
        sm.v += 16
        emit(e, lambda eng, out=out, in_=in_, sm=sm:
             eng.dma_start(out=out, in_=in_).then_inc(sm.h, 16))
        return sm.v

    def op(e, fn, sm=None, wl=(), drain=False):
        if drain:
            emit(e, lambda eng: eng.drain())
        waits(e, wl)
        if sm is not None:
            sm.v += 1
            emit(e, lambda eng, fn=fn, sm=sm: fn(eng).then_inc(sm.h, 1))
        else:
            emit(e, lambda eng, fn=fn: fn(eng))

    sGPW = [mksem(f"gpw{i}") for i in range(NR3)]
    sGS1 = [mksem(f"g1s{i}") for i in range(2)]
    sGS2 = [mksem(f"g2s{i}") for i in range(2)]
    sGRU = mksem("gru")
    sGR = mksem("gr")
    sBI = mksem("bi")
    sLXt = [mksem(f"lxt{i}") for i in range(KD)]
    sLXf = [mksem(f"lxf{i}") for i in range(5)]
    sLXc = mksem("lxc")
    sSQ = mksem("lsq")
    sLST = mksem("lst")
    sLSV = mksem("lsv")
    sLBC = mksem("lbc")
    sLCP = mksem("lcp")
    sLAPp = mksem("lapp")
    sLAP = mksem("lap")
    sPCp = [mksem(f"pcp{i}") for i in range(4)]
    sWCp = [mksem(f"wcp{i}") for i in range(4)]
    sPMM = mksem("pmm")
    sPEV = mksem("pev")
    sS, sSTT, sEXP, sNORM = mksem("as"), mksem("astt"), mksem("aexp"), mksem("anorm")
    sAT, sVT, sATC, sVTC = mksem("aat"), mksem("avt"), mksem("aatc"), mksem("avtc")
    sAV, sAOE = mksem("aav"), mksem("aaoe")
    sWMM = mksem("wmm")
    sWEV = mksem("wev")
    sWSL = [mksem(f"wsl_s{i}") for i in range(3)]
    sSGN = mksem("sgn")
    sQA = mksem("qa")
    sF1MM = mksem("f1mm")
    sSIG = mksem("sigm")
    sHID = mksem("hidm")
    sW3S = [mksem(f"w3s_s{i}") for i in range(4)]
    sSGN3 = mksem("sgn3")
    sQA3 = mksem("qa3")
    sF2MM = mksem("f2mm")
    sSILU = mksem("silu")
    sXLs = [mksem(f"xls{i}") for i in range(2)]
    sRES = mksem("res")
    sOUTs = [mksem(f"outs{i}") for i in range(2)]
    sCONST = mksem("cst")

    dma("sync", mask_sb[:], d_mask, sCONST)
    dma("sync", id_sb[:], d_id, sCONST)
    dma("sync", onesb_sb[:], d_onesb, sCONST)
    dma("sync", onescf_sb[:], d_onescf, sCONST)
    dma("sync", onesr_sb[:], d_onesr, sCONST)
    CONST_ALL = sCONST.v

    sGRv = mksem("grv")
    q1info = []
    for j in range(KH):
        q1info.append((d_w1[j], gs1[j % 2], sGS1[j % 2], gcols1[j:j + 1] if False else gcols1, j))
        q1info.append((d_w2[j], gs2[j % 2], sGS2[j % 2], gcols2, j))
    NQ1 = len(q1info)
    GRU_FULL = NQ1
    Q1 = {"pop": 0, "load": 0}
    Q1V = {}

    def _q1load(extra_wl):
        k = Q1["load"]
        if k >= NQ1:
            return
        ap, buf, sem, _, _ = q1info[k]
        wl = list(extra_wl) + ([(sGRU, k - 3)] if k - 3 > 0 else [])
        Q1V[k] = dma("sync", buf[:], ap, sem, wl=wl)
        Q1["load"] += 1

    def pop_q1(n, entry_wl=()):
        for _ in range(n):
            k = Q1["pop"]
            if k >= NQ1:
                return
            while Q1["load"] < min(k + 4, NQ1):
                _q1load(entry_wl)
            ap, buf, sem, gc, j = q1info[k]
            op("vector", lambda e, buf=buf, gc=gc, j=j: e.tensor_reduce(
                out=gc[:, j:j + 1], in_=buf[:], axis=AX, op=Alu.add,
                apply_absolute_value=True), sm=sGRU, wl=[(sem, Q1V[k])])
            Q1["pop"] += 1
    ginfo = []
    npp3 = H // 512
    for wd, gc, npp, nblk in ((d_w3, gcols3, npp3, KD),):
        for j in range(nblk):
            for p in range(npp):
                ginfo.append((wd[j, :, p * 512:(p + 1) * 512], p,
                              (gc, j, npp) if p == npp - 1 else None))
    NPIECES = len(ginfo)
    GST = {"pop": 0, "load": 0}
    GV = {}

    def _gload(extra_wl):
        k = GST["load"]
        if k >= NPIECES:
            return
        wl = list(extra_wl) + ([(sGRv, k - NR3 + 1)] if k - NR3 + 1 > 0 else [])
        GV[k] = dma("sync", gpcw[k % NR3][:], ginfo[k][0], sGPW[k % NR3], wl=wl)
        GST["load"] += 1

    def pop_gred(n, entry_wl=(), load_limit=None):
        lim = NPIECES if load_limit is None else min(load_limit, NPIECES)
        for _ in range(n):
            k = GST["pop"]
            if k >= NPIECES:
                return
            while GST["load"] < min(k + NR3 - 1, lim):
                _gload(entry_wl)
            _, p, bs = ginfo[k]
            op("vector", lambda e, k=k, p=p: e.tensor_reduce(
                out=gtmp[:, p:p + 1], in_=gpcw[k % NR3][:], axis=AX, op=Alu.add,
                apply_absolute_value=True), sm=sGRv,
               wl=[(sGPW[k % NR3], GV[k])], drain=(p == 0))
            if bs is not None:
                gc, j, npp = bs
                op("vector", lambda e, gc=gc, j=j, npp=npp: e.tensor_reduce(
                    out=gc[:, j:j + 1], in_=gtmp[:, 0:npp], axis=AX, op=Alu.add),
                   drain=True)
            GST["pop"] += 1

    def gamma_finals():
        op("vector", lambda e: e.tensor_reduce(out=g128[0][:], in_=gcols1[:], axis=AX,
                                               op=Alu.add), sm=sGR, drain=True,
           wl=[(s, s.v) for s in sGPW] + [(sGRv, sGRv.v),
               (sGS1[0], sGS1[0].v), (sGS1[1], sGS1[1].v),
               (sGS2[0], sGS2[0].v), (sGS2[1], sGS2[1].v),
               (sGRU, GRU_FULL)])
        op("vector", lambda e: e.tensor_reduce(out=g128[1][:], in_=gcols2[:], axis=AX,
                                               op=Alu.add), sm=sGR)
        op("vector", lambda e: e.tensor_reduce(out=g128[2][:], in_=gcols3[:], axis=AX,
                                               op=Alu.add), sm=sGR)
    GR_FULL = 3

    prev = dict(SGN3=0, RES=0, OUT0=0, OUT1=0, QA3=0, QA=0, WMM=0, SILU=0, AOE=0)
    BI_FULL = [0]

    for c in range(NCH):
        c0 = c * TC

        with contextlib.ExitStack() as ps:
            pssum = [ps.enter_context(nc.psum_tensor(f"pssum{c}_{h}", [1, half], F32)) for h in range(2)]
            pssq = [ps.enter_context(nc.psum_tensor(f"pssq{c}_{h}", [1, half], F32)) for h in range(2)]
            pa = [ps.enter_context(nc.psum_tensor(f"pa{c}_{h}", [128, half], F32)) for h in range(2)]
            pb = [ps.enter_context(nc.psum_tensor(f"pb{c}_{h}", [128, half], F32)) for h in range(2)]

            bSQ, bLST, bLSV, bLBC, bLAP = sSQ.v, sLST.v, sLSV.v, sLBC.v, sLAP.v
            lxv = {}
            n_w3_tiles = (4 * W3C * 4 + TCB - 1) // TCB if w3sl_off == xn_off else 0
            for ti in range(KD):
                wl = []
                if c > 0:
                    wl = [(sF1MM, prev["F1MM"])]
                    if ti < n_w3_tiles:
                        wl.append((sSGN3, prev["SGN3"]))
                lxv[ti] = dma("gpsimd", xn[:, ti, :], d_x[ti, :, c0:c0 + TC],
                              sLXt[ti], wl=wl)
            for ti in range(KD):
                if lxv[ti] is None:
                    wl = [(sLXc, bLXc + ti - (KD - NSPL) + 1), (sLST, bLST + ti)]
                else:
                    wl = [(sLXt[ti], lxv[ti]), (sLST, bLST + ti)]
                if ti == 0 and c > 0:
                    wl.append((sRES, prev["RES"]))
                    wl.append((sGR, GR_FULL))
                op("scalar", lambda e, ti=ti: e.square(out=sqb[:], in_=xn[:, ti, :]),
                   sm=sSQ, wl=wl)
                if lxv[ti] is None:
                    wl = [(sLXc, bLXc + ti - (KD - NSPL) + 1), (sSQ, bSQ + ti + 1)]
                else:
                    wl = [(sLXt[ti], lxv[ti]), (sSQ, bSQ + ti + 1)]
                if ti == 0:
                    if c == 0:
                        wl.append((sCONST, CONST_ALL))
                    else:
                        wl.append((sSILU, prev["SILU"]))
                        wl.append((sAOE, prev["AOE"]))
                        wl.append((sBI, BI_FULL[0]))
                waits("tensor", wl)
                for h in range(2):
                    op("tensor", lambda e, ti=ti, h=h: e.matmul(
                        out=pssum[h][:], lhsT=onesb_sb[:], rhs=xn[:, ti, h * half:(h + 1) * half],
                        start=(ti == 0), stop=(ti == KD - 1)))
                for h in range(2):
                    op("tensor", lambda e, ti=ti, h=h: e.matmul(
                        out=pssq[h][:], lhsT=onesb_sb[:], rhs=sqb[:, h * half:(h + 1) * half],
                        start=(ti == 0), stop=(ti == KD - 1)),
                       sm=(sLST if h == 1 else None))
            for h in range(2):
                waits("vector", [(sLST, bLST + KD)])
                op("vector", lambda e, h=h: e.tensor_scalar(
                    out=st_negm[:], in0=pssum[h][:], scalar1=-1.0 / D, scalar2=None, op0=Alu.mult),
                   drain=True)
                op("vector", lambda e, h=h: e.tensor_scalar(
                    out=st_esq[:], in0=pssq[h][:], scalar1=1.0 / D, scalar2=None, op0=Alu.mult),
                   drain=True)
                op("vector", lambda e: e.tensor_tensor(
                    out=st_tbuf[:], in0=st_negm[:], in1=st_negm[:], op=Alu.mult), drain=True)
                op("vector", lambda e: e.tensor_tensor(
                    out=st_esq[:], in0=st_esq[:], in1=st_tbuf[:], op=Alu.subtract), drain=True)
                op("vector", lambda e: e.tensor_scalar(
                    out=st_esq[:], in0=st_esq[:], scalar1=EPS_LN, scalar2=None, op0=Alu.add),
                   drain=True)
                op("vector", lambda e: e.reciprocal(out=st_tbuf[:], in_=st_esq[:]), sm=sLSV,
                   drain=True)
                op("scalar", lambda e: e.sqrt(out=st_a[:], in_=st_tbuf[:]),
                   sm=sLSV, wl=[(sLSV, bLSV + 3 * h + 1), (sLBC, bLBC + 2 * h)])
                op("vector", lambda e: e.tensor_tensor(
                    out=st_tbuf[:], in0=st_negm[:], in1=st_a[:], op=Alu.mult),
                   sm=sLSV, wl=[(sLSV, bLSV + 3 * h + 2)], drain=True)
                waits("tensor", [(sLSV, bLSV + 3 * h + 3)])
                op("tensor", lambda e, h=h: e.matmul(
                    out=pa[h][:], lhsT=onesr_sb[:], rhs=st_a[:], start=True, stop=True),
                   sm=sLBC)
                op("tensor", lambda e, h=h: e.matmul(
                    out=pb[h][:], lhsT=onesr_sb[:], rhs=st_tbuf[:], start=True, stop=True),
                   sm=sLBC)
                if h == 0:
                    waits("vector", [(sLBC, bLBC + 2)])
            pcv = {}
            wds = [d_wq] * KD + [d_wk] * KD + [d_wv] * KD
            for b in range(min(4, 3 * KD)):
                wl = []
                if c > 0:
                    wl = [(sQA3, prev["QA3"]), (sQA, prev["QA"]), (sRES, prev["RES"]),
                          (sOUTs[0], prev["OUT0"]), (sOUTs[1], prev["OUT1"]),
                          (sAOE, prev["AOE"])]
                pcv[b] = dma("gpsimd", pcst[b % 4][:], wds[b][b % KD], sPCp[b % 4], wl=wl)
            NPOOL = max(0, KD * 5 // 16)
            bLCP, bLAPp = sLCP.v, sLAPp.v
            for h in range(2):
                wl = [(sLBC, bLBC + 2 * (h + 1))]
                if h == 1:
                    wl.append((sLAPp, bLAPp + NPOOL))
                op("vector", lambda e, h=h: e.tensor_copy(out=pa_sb[:], in_=pa[h][:]),
                   sm=sLCP, wl=wl, drain=True)
                op("vector", lambda e, h=h: e.tensor_copy(out=pb_sb[:], in_=pb[h][:]),
                   sm=sLCP, drain=True)
                for ti in range(KD - NPOOL, KD):
                    wl = [(sLCP, bLCP + 2 * (h + 1))]
                    if c > 0 and ti == KD - NPOOL and h == 0:
                        wl.append((sRES, prev["RES"]))
                        wl.append((sGR, GR_FULL))
                    op("gpsimd", lambda e, ti=ti, h=h: e.tensor_tensor(
                        out=xn[:, ti, h * half:(h + 1) * half],
                        in0=xn[:, ti, h * half:(h + 1) * half],
                        in1=pa_sb[:], op=Alu.mult), wl=wl, drain=True)
                    op("gpsimd", lambda e, ti=ti, h=h: e.tensor_tensor(
                        out=xn[:, ti, h * half:(h + 1) * half],
                        in0=xn[:, ti, h * half:(h + 1) * half],
                        in1=pb_sb[:], op=Alu.add), sm=sLAPp, drain=True)
            for ti in range(KD - NPOOL):
                for h in range(2):
                    waits("vector", [(sLBC, bLBC + 2 * (h + 1))])
                    op("vector", lambda e, ti=ti, h=h: e.tensor_tensor(
                        out=sqb[:, 0:half], in0=xn[:, ti, h * half:(h + 1) * half],
                        in1=pa[h][:], op=Alu.mult), drain=True)
                    op("vector", lambda e, ti=ti, h=h: e.tensor_tensor(
                        out=xn[:, ti, h * half:(h + 1) * half], in0=sqb[:, 0:half],
                        in1=pb[h][:], op=Alu.add), sm=sLAP, drain=True)
            LAP_FULL = sLAP.v
            LAPP_FULL = sLAPp.v

        with contextlib.ExitStack() as ps:
            pacc = [ps.enter_context(nc.psum_tensor(f"pacc{c}_{i}", [128, TC], F32)) for i in range(2)]
            bPMM, bPEV = sPMM.v, sPEV.v
            LBC_FULL = sLBC.v
            b = 0
            for kind, wd in (("q", d_wq), ("k", d_wk), ("v", d_wv)):
                for j in range(KD):
                    if b in pcv:
                        vc = pcv[b]
                    else:
                        wl = [(sPMM, bPMM + b - 3)]
                        vc = dma("gpsimd", pcst[b % 4][:], wd[j], sPCp[b % 4], wl=wl)
                    wl = [(sPCp[b % 4], vc), (sPEV, bPEV + b - 1)]
                    if b == 0:
                        wl.append((sLAP, LAP_FULL))
                        wl.append((sLAPp, LAPP_FULL))
                    waits("tensor", wl)
                    for i in range(KD):
                        for n0 in range(0, TC, NW512):
                            op("tensor", lambda e, b=b, i=i, n0=n0: e.matmul(
                                out=pacc[b % 2][:, n0:n0 + NW512],
                                lhsT=pcst[b % 4][:, i * 128:(i + 1) * 128],
                                rhs=xn[:, i, n0:n0 + NW512],
                                start=(i == 0), stop=(i == KD - 1)),
                               sm=(sPMM if (i == KD - 1 and n0 + NW512 >= TC) else None))
                    dst = {"q": q_sb, "k": k_sb, "v": v_sb}[kind]
                    op("vector", lambda e, b=b, j=j, dst=dst: e.tensor_copy(
                        out=dst[:, :, j * G:(j + 1) * G],
                        in_=pacc[b % 2][:].rearrange("p (g t) -> p g t", t=G)),
                       sm=sPEV, wl=[(sPMM, bPMM + b + 1)])
                    if c == 0:
                        pop_q1(2)
                    b += 1
            PMM_FULL = sPMM.v

        with contextlib.ExitStack() as ps:
            pS = [ps.enter_context(nc.psum_tensor(f"pS{c}_{i}", [128, 128], F32)) for i in range(2)]
            pAT = [ps.enter_context(nc.psum_tensor(f"pAT{c}_{i}", [128, 128], BF16)) for i in range(2)]
            pVT = [ps.enter_context(nc.psum_tensor(f"pVT{c}_{i}", [128, 128], BF16)) for i in range(2)]
            pAV = [ps.enter_context(nc.psum_tensor(f"pAV{c}_{i}", [128, 128], F32)) for i in range(2)]
            bS, bSTT, bEXP, bNORM = sS.v, sSTT.v, sEXP.v, sNORM.v
            bAT, bVT, bATC, bVTC = sAT.v, sVT.v, sATC.v, sVTC.v
            bAV, bAOE = sAV.v, sAOE.v
            PEV_FULL = sPEV.v
            ISQ = float(1.0 / np.sqrt(128.0))

            for g in range(NG + 7):
                if g < NG:
                    gg = g
                    wl = [(sSTT, bSTT + gg - 1)]
                    if gg == 0:
                        wl.append((sPEV, PEV_FULL))
                    op("tensor", lambda e, gg=gg: e.matmul(
                        out=pS[gg % 2][:], lhsT=q_sb[:, gg, :], rhs=k_sb[:, gg, :],
                        start=True, stop=True), sm=sS, wl=wl)
                if 0 <= g - 1 < NG:
                    gg = g - 1
                    wl = [(sS, bS + gg + 1), (sEXP, bEXP + gg - 1)]
                    if gg == 0 and c == 0:
                        wl.append((sCONST, CONST_ALL))
                        wl.append((sGRv, sGRv.v))
                    op("vector", lambda e, gg=gg: e.scalar_tensor_tensor(
                        out=tmpw[gg % 2][:], in0=pS[gg % 2][:], scalar=ISQ,
                        in1=mask_sb[:], op0=Alu.mult, op1=Alu.add),
                       sm=sSTT, wl=wl)
                if 0 <= g - 2 < NG:
                    gg = g - 2
                    wl = [(sSTT, bSTT + gg + 1), (sNORM, bNORM + gg - 1)]
                    if gg == 0 and c == 0:
                        wl.append((sGRv, sGRv.v))
                    op("scalar", lambda e, gg=gg: e.activation(
                        out=Ab[gg % 2][:], in_=tmpw[gg % 2][:], func=Act.Exp,
                        accum_out=rs[gg % 2][:]),
                       sm=sEXP, wl=wl)
                if 0 <= g - 3 < NG:
                    gg = g - 3
                    waits("vector", [(sEXP, bEXP + gg + 1), (sAT, bAT + gg - 1)])
                    op("vector", lambda e, gg=gg: e.reciprocal(out=rr[gg % 2][:], in_=rs[gg % 2][:]))
                    op("vector", lambda e, gg=gg: e.tensor_scalar(
                        out=An[gg % 2][:], in0=Ab[gg % 2][:], scalar1=rr[gg % 2][:],
                        scalar2=None, op0=Alu.mult), sm=sNORM, drain=True)
                if 0 <= g - 4 < NG:
                    gg = g - 4
                    op("tensor", lambda e, gg=gg: e.transpose(
                        out=pAT[gg % 2][:], in_=An[gg % 2][:], identity=id_sb[:]),
                       sm=sAT, wl=[(sNORM, bNORM + gg + 1), (sATC, bATC + gg - 1)])
                    op("tensor", lambda e, gg=gg: e.transpose(
                        out=pVT[gg % 2][:], in_=v_sb[:, gg, :], identity=id_sb[:]),
                       sm=sVT, wl=[(sVTC, bVTC + gg - 1)])
                if 0 <= g - 5 < NG:
                    gg = g - 5
                    op("scalar", lambda e, gg=gg: e.copy(out=ATs[gg % 2][:], in_=pAT[gg % 2][:]),
                       sm=sATC, wl=[(sAT, bAT + gg + 1), (sAV, bAV + gg - 1)])
                    op("vector", lambda e, gg=gg: e.tensor_copy(out=VTs[gg % 2][:], in_=pVT[gg % 2][:]),
                       sm=sVTC, wl=[(sVT, bVT + gg + 1), (sAV, bAV + gg - 1)])
                if 0 <= g - 6 < NG:
                    gg = g - 6
                    op("tensor", lambda e, gg=gg: e.matmul(
                        out=pAV[gg % 2][:], lhsT=VTs[gg % 2][:], rhs=ATs[gg % 2][:],
                        start=True, stop=True),
                       sm=sAV, wl=[(sATC, bATC + gg + 1), (sVTC, bVTC + gg + 1),
                                   (sAOE, bAOE + gg - 1)])
                if 0 <= g - 7 < NG:
                    gg = g - 7
                    wl = [(sAV, bAV + gg + 1)]
                    if gg == 0 and c > 0:
                        wl.append((sWMM, prev["WMM"]))
                        wl.append((sRES, prev["RES"]))
                        wl.append((sOUTs[0], prev["OUT0"]))
                        wl.append((sOUTs[1], prev["OUT1"]))
                    op("scalar", lambda e, gg=gg: e.copy(
                        out=ao_sb[:, :, gg * G:(gg + 1) * G],
                        in_=pAV[gg % 2][:].rearrange("p (h t) -> p h t", t=G)),
                       sm=sAOE, wl=wl)
            S_FULL, VT_FULL, AOE_FULL = sS.v, sVT.v, sAOE.v
            EXP_FULL, NORM_FULL, AT_FULL, AV_FULL = sEXP.v, sNORM.v, sAT.v, sAV.v

        with contextlib.ExitStack() as ps:
            pacc = [ps.enter_context(nc.psum_tensor(f"wacc{c}_{i}", [128, TC], F32)) for i in range(2)]
            bWMM, bWEV = sWMM.v, sWEV.v
            for j in range(KD):
                wl = [(sWMM, bWMM + j - 3)]
                if j <= 3:
                    wl.append((sS, S_FULL))
                vc = dma("gpsimd", wcstw[j % 4][:], d_wo[j], sWCp[j % 4], wl=wl)
                wl = [(sWCp[j % 4], vc), (sWEV, bWEV + j - 1)]
                if j == 0:
                    wl.append((sAOE, AOE_FULL))
                waits("tensor", wl)
                for i in range(KD):
                    for n0 in range(0, TC, NW512):
                        op("tensor", lambda e, j=j, i=i, n0=n0: e.matmul(
                            out=pacc[j % 2][:, n0:n0 + NW512],
                            lhsT=wcstw[j % 4][:, i * 128:(i + 1) * 128],
                            rhs=ao_sb[:, i, n0:n0 + NW512],
                            start=(i == 0), stop=(i == KD - 1)),
                           sm=(sWMM if (i == KD - 1 and n0 + NW512 >= TC) else None))
                wl = [(sWMM, bWMM + j + 1)]
                if j == 0 and c == 0:
                    pop_q1(NQ1 + 1)
                    wl.append((sGRU, GRU_FULL))
                if j == 0 and c > 0:
                    wl.append((sOUTs[0], prev["OUT0"]))
                    wl.append((sOUTs[1], prev["OUT1"]))
                op("vector", lambda e, j=j: e.tensor_copy(
                    out=s_sb[:, j, :], in_=pacc[j % 2][:]), sm=sWEV, wl=wl)
                if c == 0:
                    pop_q1(NQ1 + 1)
                    aw = [(sEXP, EXP_FULL), (sNORM, NORM_FULL), (sAT, AT_FULL),
                          (sAV, AV_FULL), (sS, S_FULL)]
                    pop_gred(12 if j < KD - 1 else NPIECES + 1, entry_wl=aw)
                    if j == KD - 1:
                        gamma_finals()
            WMM_FULL = sWMM.v
            WEV_FULL = sWEV.v

            if c == 0:
                pgs = ps.enter_context(nc.psum_tensor("pgs", [1, 1], F32))
                pgb = ps.enter_context(nc.psum_tensor("pgb", [128, 1], F32))
                bBI = sBI.v
                for m in range(3):
                    op("tensor", lambda e, m=m: e.matmul(
                        out=pgs[:], lhsT=g128[m][:], rhs=onescf_sb[:], start=True, stop=True),
                       sm=sBI, wl=[(sGR, GR_FULL), (sBI, bBI + 4 * m)])
                    op("vector", lambda e, m=m: e.tensor_scalar(
                        out=gbv[m][:], in0=pgs[:], scalar1=0.5 / NW, scalar2=EPS_Q * 0.5,
                        op0=Alu.mult, op1=Alu.add), sm=sBI, wl=[(sBI, bBI + 4 * m + 1)])
                    op("tensor", lambda e, m=m: e.matmul(
                        out=pgb[:], lhsT=onesr_sb[:], rhs=gbv[m][:], start=True, stop=True),
                       sm=sBI, wl=[(sBI, bBI + 4 * m + 2)])
                    waits("vector", [(sBI, bBI + 4 * m + 3)])
                    op("vector", lambda e, m=m: e.tensor_copy(out=bpos[m][:], in_=pgb[:]))
                    op("vector", lambda e, m=m: e.tensor_scalar(
                        out=bneg[m][:], in0=bpos[m][:], scalar1=-1.0, scalar2=None,
                        op0=Alu.mult), sm=sBI, drain=True)
                BI_FULL[0] = sBI.v

        with contextlib.ExitStack() as ps:
            px1 = [ps.enter_context(nc.psum_tensor(f"px1_{c}_{i}", [128, TC], F32)) for i in range(2)]
            px2 = [ps.enter_context(nc.psum_tensor(f"px2_{c}_{i}", [128, TC], F32)) for i in range(2)]
            bSGN, bQA, bF1MM, bSIG, bHID = \
                sSGN.v, sQA.v, sF1MM.v, sSIG.v, sHID.v

            def f1_epilogue(j):
                wl = [(sF1MM, bF1MM + 2 * j + 1), (sHID, bHID + j - 1)]
                if j <= 1 and c == 0:
                    wl.append((sGR, GR_FULL))
                if j <= 1 and c > 0:
                    wl.append((sRES, prev["RES"]))
                op("scalar", lambda e, j=j: e.activation(
                    out=sig[j % 2][:], in_=px1[j % 2][:], func=Act.Sigmoid, scale=0.5),
                   sm=sSIG, wl=wl)
                op("vector", lambda e, j=j: e.tensor_tensor(
                    out=hid[:, j, :], in0=sig[j % 2][:], in1=px2[j % 2][:], op=Alu.mult),
                   sm=sHID, wl=[(sSIG, bSIG + j + 1), (sF1MM, bF1MM + 2 * j + 2)])

            for j in range(KH):
                wl1 = [(sSGN, bSGN + 4 * j - 4)]
                wl2 = [(sSGN, bSGN + 4 * j - 2)]
                if j == 0:
                    entry = [(sS, S_FULL), (sVT, VT_FULL), (sWMM, WMM_FULL),
                             (sGR, GR_FULL)]
                    if c > 0:
                        entry += [(sRES, prev["RES"]), (sOUTs[0], prev["OUT0"]),
                                  (sOUTs[1], prev["OUT1"])]
                    wl1 = entry
                    wl2 = entry
                vl1 = dma("sync", wsl[(2 * j) % 3][:], d_w1[j], sWSL[(2 * j) % 3], wl=wl1)
                vl2 = dma("scalar", wsl[(2 * j + 1) % 3][:], d_w2[j], sWSL[(2 * j + 1) % 3], wl=wl2)
                for t, ldw in ((2 * j, (sWSL[(2 * j) % 3], vl1)),
                               (2 * j + 1, (sWSL[(2 * j + 1) % 3], vl2))):
                    wl = [ldw, (sF1MM, bF1MM + t - 1), (sQA, bQA + t - 1)]
                    if t == 0:
                        wl.append((sBI, bBI + 8 if c == 0 else BI_FULL[0]))
                        wl.append((sWMM, WMM_FULL))
                        wl.append((sS, S_FULL))
                        wl.append((sVT, VT_FULL))
                    bias_p = bpos[0] if t % 2 == 0 else bpos[1]
                    bias_n = bneg[0] if t % 2 == 0 else bneg[1]
                    wl_b = [w for w in wl if w[0] is not sF1MM]
                    waits("scalar", wl_b)
                    op("scalar", lambda e, t=t, bb=bias_p: e.sign(
                        out=qsbt[t % 2][:], in_=wsl[t % 3][:], bias=bb[:]), sm=sSGN)
                    waits("scalar", [w for w in wl if w[0] is sF1MM])
                    op("scalar", lambda e, t=t, bb=bias_n: e.sign(
                        out=q12[t % 2][:], in_=wsl[t % 3][:], bias=bb[:]), sm=sSGN)
                    wl = [(sSGN, bSGN + 2 * t + 2)]
                    if t == 0:
                        wl.append((sS, S_FULL))
                        wl.append((sVT, VT_FULL))
                    op("vector", lambda e, t=t: e.tensor_tensor(
                        out=q12[t % 2][:], in0=q12[t % 2][:], in1=qsbt[t % 2][:],
                        op=Alu.add), sm=sQA, wl=wl)
                wl = [(sQA, bQA + 2 * j + 1), (sSIG, bSIG + j - 1)]
                if j == 0:
                    wl.append((sWEV, WEV_FULL))
                waits("tensor", wl)
                for i in range(KD):
                    for n0 in range(0, TC, NW512):
                        op("tensor", lambda e, j=j, i=i, n0=n0: e.matmul(
                            out=px1[j % 2][:, n0:n0 + NW512],
                            lhsT=q12[0][:, i * 128:(i + 1) * 128],
                            rhs=xn[:, i, n0:n0 + NW512],
                            start=(i == 0), stop=(i == KD - 1)),
                           sm=(sF1MM if (i == KD - 1 and n0 + NW512 >= TC) else None))
                waits("tensor", [(sQA, bQA + 2 * j + 2), (sHID, bHID + j - 1)])
                for i in range(KD):
                    for n0 in range(0, TC, NW512):
                        op("tensor", lambda e, j=j, i=i, n0=n0: e.matmul(
                            out=px2[j % 2][:, n0:n0 + NW512],
                            lhsT=q12[1][:, i * 128:(i + 1) * 128],
                            rhs=xn[:, i, n0:n0 + NW512],
                            start=(i == 0), stop=(i == KD - 1)),
                           sm=(sF1MM if (i == KD - 1 and n0 + NW512 >= TC) else None))
                if j >= 1:
                    f1_epilogue(j - 1)
            f1_epilogue(KH - 1)
            F1MM_FULL = sF1MM.v
            SIG_FULL = sSIG.v
            HID_FULL = sHID.v
            QA_FULL = sQA.v
            SGN_FULL = sSGN.v

        with contextlib.ExitStack() as ps:
            px3 = [ps.enter_context(nc.psum_tensor(f"px3_{c}_{i}", [128, TC], F32)) for i in range(2)]
            bSGN3, bQA3, bF2MM, bSILU, bRES = \
                sSGN3.v, sQA3.v, sF2MM.v, sSILU.v, sRES.v
            w3v = {}
            xv = {}
            xvh = {}
            last_out = {0: sOUTs[0].v, 1: sOUTs[1].v}

            def f2_epilogue(j):
                wl = [(sF2MM, bF2MM + NQT * (j + 1)), (sRES, bRES + 4 * j - 5)]
                op("scalar", lambda e, j=j: e.activation(
                    out=gsb[j % 2][:], in_=px3[j % 2][:], func=Act.Silu, scale=0.25),
                   sm=sSILU, wl=wl)
                for hh in range(2):
                    wlm = [(sSILU, bSILU + j + 1), (sOUTs[hh], last_out[hh]),
                           (sWEV, WEV_FULL)]
                    if j == 0:
                        wlm.append((sSGN, SGN_FULL))
                    op("vector", lambda e, j=j, hh=hh: e.tensor_tensor(
                        out=ostg[hh][:], in0=s_sb[:, j, hh * half:(hh + 1) * half],
                        in1=gsb[j % 2][:, hh * half:(hh + 1) * half], op=Alu.mult),
                       sm=sRES, wl=wlm)
                    op("vector", lambda e, j=j, hh=hh: e.tensor_tensor(
                        out=ostg[hh][:], in0=ostg[hh][:], in1=xstg[hh][:], op=Alu.add),
                       sm=sRES, wl=[(sXLs[hh], xvh[(j, hh)])], drain=True)
                    last_out[hh] = dma(
                        "gpsimd", d_out[j, :, c0 + hh * half:c0 + (hh + 1) * half],
                        ostg[hh][:], sOUTs[hh], wl=[(sRES, bRES + 4 * j + 2 * hh + 2)])

            for j in range(KD):
                for qt in range(NQT):
                    t = j * NQT + qt
                    qe = "sync" if t % 2 == 0 else "scalar"
                    wl = [(sSGN3, bSGN3 + 2 * t - 6)]
                    if t <= 3:
                        wl = [(sF1MM, F1MM_FULL)]
                    w3v[t] = dma(qe, w3sl[t % 4][:], d_w3[j, :, qt * W3C:(qt + 1) * W3C],
                                 sW3S[t % 4], wl=wl)
                for hh in range(2):
                    wl = [(sRES, bRES + 4 * (j - 1) + 2 * hh + 2)]
                    if j == 0:
                        wl = [(sWMM, WMM_FULL), (sSGN, SGN_FULL)] + \
                             [(s, s.v) for s in sPCp]
                    xvh[(j, hh)] = dma("gpsimd", xstg[hh][:],
                                       d_x[j, :, c0 + hh * half:c0 + (hh + 1) * half],
                                       sXLs[hh], wl=wl)
                for qt in range(NQT):
                    t = j * NQT + qt
                    ld = (sW3S[t % 4], w3v[t])
                    wl = [ld, (sF2MM, bF2MM + t - 2), (sQA3, bQA3 + t - 1)]
                    if t == 0:
                        wl.append((sBI, BI_FULL[0]))
                        wl.append((sQA, QA_FULL))
                    waits("scalar", wl)
                    op("scalar", lambda e, t=t: e.sign(
                        out=q3sl[t % 3][:], in_=w3sl[t % 4][:], bias=bneg[2][:]), sm=sSGN3)
                    op("scalar", lambda e, t=t: e.sign(
                        out=q3sbt[t % 2][:], in_=w3sl[t % 4][:], bias=bpos[2][:]), sm=sSGN3)
                    op("vector", lambda e, t=t: e.tensor_tensor(
                        out=q3sl[t % 3][:], in0=q3sl[t % 3][:], in1=q3sbt[t % 2][:],
                        op=Alu.add), sm=sQA3, wl=[(sSGN3, bSGN3 + 2 * t + 2)])
                wl0 = [(sSILU, bSILU + j - 1)]
                if j == 0:
                    wl0.append((sSIG, SIG_FULL))
                    wl0.append((sHID, HID_FULL))
                waits("tensor", wl0)
                for i in range(KH):
                    qi = i // KHQ
                    t = j * NQT + qi
                    if i % KHQ == 0:
                        waits("tensor", [(sQA3, bQA3 + t + 1)])
                    for n0 in range(0, TC, NW512):
                        op("tensor", lambda e, j=j, i=i, t=t, n0=n0: e.matmul(
                            out=px3[j % 2][:, n0:n0 + NW512],
                            lhsT=q3sl[t % 3][:, (i % KHQ) * 128:(i % KHQ + 1) * 128],
                            rhs=hid[:, i, n0:n0 + NW512],
                            start=(i == 0), stop=(i == KH - 1)),
                           sm=(sF2MM if ((i + 1) % KHQ == 0 and n0 + NW512 >= TC) else None))
                if j >= 1:
                    f2_epilogue(j - 1)
            f2_epilogue(KD - 1)
            prev = dict(SGN3=sSGN3.v, RES=sRES.v, OUT0=sOUTs[0].v, OUT1=sOUTs[1].v,
                        QA3=sQA3.v, QA=sQA.v, WMM=WMM_FULL, SILU=sSILU.v, AOE=AOE_FULL,
                        F1MM=F1MM_FULL)

    waits("gpsimd", [(sOUTs[0], sOUTs[0].v), (sOUTs[1], sOUTs[1].v)])

    with nc.Block() as block:
        @block.sync
        def _(eng):
            for fn in prog["sync"]:
                fn(eng)

        @block.tensor
        def _(eng):
            for fn in prog["tensor"]:
                fn(eng)

        @block.vector
        def _(eng):
            for fn in prog["vector"]:
                fn(eng)

        @block.scalar
        def _(eng):
            for fn in prog["scalar"]:
                fn(eng)

        @block.gpsimd
        def _(eng):
            for fn in prog["gpsimd"]:
                fn(eng)

    stack.close()
    return nc


def tile_w(W):
    O, I = W.shape
    KI, KO = I // 128, O // 128
    A = np.ascontiguousarray(np.asarray(W, dtype=np.float32).T)
    T = A.reshape(KI, 128, KO, 128).transpose(2, 1, 0, 3).reshape(KO, 128, KI * 128)
    return np.ascontiguousarray(T)


def host_inputs(P, x, Wq, Wk, Wv, Wo, w1, w2, w3):
    D, NH, T_core = P["D"], P["NH"], P["T_core"]
    KD = D // 128
    G = 128 // NH
    n_cores = P["n_cores"]
    tokens = np.ascontiguousarray(np.asarray(x, dtype=np.float32).reshape(-1, D))
    assert tokens.shape[0] == n_cores * T_core
    wq_t, wk_t, wv_t, wo_t = tile_w(Wq), tile_w(Wk), tile_w(Wv), tile_w(Wo)
    w1_t, w2_t, w3_t = tile_w(w1), tile_w(w2), tile_w(w3)
    idx = np.arange(128)
    mask = np.where((idx[:, None] % G) == (idx[None, :] % G), 0.0, NEG).astype(np.float32)
    idm = np.eye(128).astype(BF)
    onesb = np.ones((128, 1)).astype(BF)
    onescf = np.ones((128, 1), dtype=np.float32)
    onesr = np.ones((1, 128), dtype=np.float32)
    in_maps = []
    for cix in range(n_cores):
        shard = tokens[cix * T_core:(cix + 1) * T_core]
        x_fm = np.ascontiguousarray(shard.T).reshape(KD, 128, T_core)
        in_maps.append({"x": x_fm, "wq": wq_t, "wk": wk_t, "wv": wv_t, "wo": wo_t,
                        "w1": w1_t, "w2": w2_t, "w3": w3_t, "mask": mask, "idm": idm,
                        "onesb": onesb, "onescf": onescf, "onesr": onesr})
    return in_maps


def assemble_out(P, shards, out_shape):
    D, T_core = P["D"], P["T_core"]
    toks = []
    for r in shards:
        o = np.asarray(r["out"], dtype=np.float32).reshape(D, T_core)
        toks.append(o.T)
    return np.ascontiguousarray(np.concatenate(toks, axis=0).reshape(out_shape))


def kernel(L0_attn_x, L0_attn_Wq, L0_attn_Wk, L0_attn_Wv, L0_attn_Wo,
           L0_ffn_w1, L0_ffn_w2, L0_ffn_w3):
    P = FULL
    in_maps = host_inputs(P, L0_attn_x, L0_attn_Wq, L0_attn_Wk, L0_attn_Wv,
                          L0_attn_Wo, L0_ffn_w1, L0_ffn_w2, L0_ffn_w3)
    nc = build_program(P)
    res = run_bass_kernel_spmd(nc, in_maps, core_ids=list(range(P["n_cores"])))
    return assemble_out(P, res.results, np.asarray(L0_attn_x).shape)
